# version 28
# baseline (speedup 1.0000x reference)
import numpy as np
import concourse.bass as bass
import concourse.bass_utils as _bu
import concourse.mybir as mybir
from concourse.bass_utils import run_bass_kernel_spmd

if not getattr(_bu, "_ldw_patched", False):
    _orig_run_command = _bu.run_command

    def _run_command_ldw(argv, **kwargs):
        argv = ["--enable-ldw-opt=true" if a == "--enable-ldw-opt=false" else a
                for a in argv]
        return _orig_run_command(argv, **kwargs)

    _bu.run_command = _run_command_ldw
    _bu._ldw_patched = True

F32 = mybir.dt.float32
BF16 = mybir.dt.bfloat16
F16 = mybir.dt.float16
I32 = mybir.dt.int32
AF = mybir.ActivationFunctionType
ALU = mybir.AluOpType
AX = mybir.AxisListType

NCORE = 8
D = 2048
KC = D // 128
T = 1024
NT = T // 128
IFF = 5632
OCG = 2 * IFF // 128
ICD = IFF // 128
NH2 = 16
EPS_LN = 1e-5
EPS_RMS = 1e-6
CLIP = 1e-6

DQ_SUB_I32 = 128
GU_SPLITS = [(0, 2), (2, 4), (4, 6), (6, 8), (8, 11)]


class Buf:
    __slots__ = ("name", "wtok", "rtoks")

    def __init__(self, name):
        self.name = name
        self.wtok = None
        self.rtoks = {}


class Ctx:
    def __init__(self, nc):
        self.nc = nc
        self.sems = {}
        self.counts = {}
        names = ["pe", "dv", "sc", "gp", "qcc0", "qcc1"]
        names += [f"qsy{i}" for i in range(8)] + [f"qgp{i}" for i in range(8)]
        names += [f"qsc{i}" for i in range(8)]
        for s in names:
            self.sems[s] = nc.alloc_semaphore(s)
            self.counts[s] = 0
        self.qrr = {"qsy": 0, "qgp": 0, "qsc": 0, "qcc": 0}
        self.streams = {"pe": [], "dv": [], "sc": [], "gp": [], "sy": []}
        self.marks = {e: {} for e in self.streams}
        self.eng_sem = {"pe": "pe", "dv": "dv", "sc": "sc", "gp": "gp", "sy": None}

    def _wait(self, stream, sem, val):
        if val <= 0:
            return
        if self.marks[stream].get(sem, 0) >= val:
            return
        self.marks[stream][sem] = val
        s = self.sems[sem]
        self.streams[stream].append(lambda eng, s=s, val=val: eng.wait_ge(s, val))

    def op(self, stream, fn, reads=(), writes=(), sem=None, inc=0, raw_deps=()):
        need = {}

        def add(tok):
            if tok is None:
                return
            s, v = tok
            if need.get(s, 0) < v:
                need[s] = v

        for b in reads:
            add(b.wtok)
        for b in writes:
            add(b.wtok)
            for s, v in b.rtoks.items():
                add((s, v))
        for tok in raw_deps:
            add(tok)
        for s, v in sorted(need.items()):
            self._wait(stream, s, v)

        tok = None
        if sem is not None and inc:
            self.counts[sem] += inc
            tok = (sem, self.counts[sem])
            semh = self.sems[sem]

            def thunk(eng, fn=fn, semh=semh, cnt=inc):
                fn(eng).then_inc(semh, cnt)
            self.streams[stream].append(thunk)
        else:
            self.streams[stream].append(lambda eng, fn=fn: fn(eng))

        if tok is not None:
            for b in writes:
                b.wtok = tok
                b.rtoks = {}
            for b in reads:
                s, v = tok
                if b.rtoks.get(s, 0) < v:
                    b.rtoks[s] = v
        return tok

    def barrier(self):
        snap = dict(self.counts)
        for stream in self.streams:
            own = self.eng_sem[stream]
            for s, v in snap.items():
                if s != own:
                    self._wait(stream, s, v)

    def emit(self):
        with self.nc.Block() as block:
            block.sync(lambda eng: [t(eng) for t in self.streams["sy"]])
            block.tensor(lambda eng: [t(eng) for t in self.streams["pe"]])
            block.vector(lambda eng: [t(eng) for t in self.streams["dv"]])
            block.scalar(lambda eng: [t(eng) for t in self.streams["sc"]])
            block.gpsimd(lambda eng: [t(eng) for t in self.streams["gp"]])


class Arena:
    def __init__(self, nc, total_bytes):
        self.t = nc.alloc_sbuf_tensor("arena", [128, total_bytes // 4], F32)
        self.base = self.t.ap()
        self.off = 0
        self.total = total_bytes
        self.stack = []
        self.peak = 0

    def push(self):
        self.stack.append(self.off)

    def pop(self):
        self.off = self.stack.pop()

    def alloc(self, cols, dtype, rows=128):
        esz = {F32: 4, I32: 4, BF16: 2, F16: 2}[dtype]
        self.off = (self.off + 3) // 4 * 4
        start = self.off
        self.off += cols * esz
        self.peak = max(self.peak, self.off)
        assert self.off <= self.total, f"arena overflow: {self.off} > {self.total}"
        c0 = start // 4
        if dtype is F32:
            ap = self.base[:, c0:c0 + cols]
        elif dtype is I32:
            ap = self.base[:, c0:c0 + cols].bitcast(I32)
        else:
            n4 = (cols + 1) // 2
            ap = self.base[:, c0:c0 + n4].bitcast(dtype)[:, 0:cols]
        if rows != 128:
            ap = ap[0:rows, :]
        return ap


def build_program(phases=("ffn1", "attn", "ffn3"), dumps=()):
    nc = bass.Bass("TRN2", target_bir_lowering=False, num_devices=NCORE)

    x_in = nc.dram_tensor("x_in", [T, D], F32, kind="ExternalInput").ap()
    gup_in = nc.dram_tensor("gup", [1408, 256], I32, kind="ExternalInput").ap()
    gus_in = nc.dram_tensor("gus", [1408, 16], F16, kind="ExternalInput").ap()
    dnp_in = nc.dram_tensor("dnp", [D, 96], I32, kind="ExternalInput").ap()
    dns_in = nc.dram_tensor("dns", [D, 6], F16, kind="ExternalInput").ap()
    wq_in = nc.dram_tensor("wq", [D, 256], F32, kind="ExternalInput").ap()
    wk_in = nc.dram_tensor("wk", [D, 256], F32, kind="ExternalInput").ap()
    wv_in = nc.dram_tensor("wv", [D, 256], F32, kind="ExternalInput").ap()
    wo_in = nc.dram_tensor("wo", [256, D], F32, kind="ExternalInput").ap()
    bo_in = nc.dram_tensor("bo", [1, D], F32, kind="ExternalInput").ap()
    selb_in = nc.dram_tensor("selb", [1, 2], F32, kind="ExternalInput").ap()
    y_out = nc.dram_tensor("y", [T, D], F32, kind="ExternalOutput").ap()

    wsh_all = nc.dram_tensor("wsh_all", [8, 262144], BF16, kind="Internal").ap()
    W_IDX = {"wqt": 0, "wkt": 1, "wvt": 2, "wot": 3}
    wsh = {w: wsh_all[W_IDX[w] * 2:W_IDX[w] * 2 + 2, :]
           for w in ("wqt", "wkt", "wvt", "wot")}
    gu_sh = [nc.dram_tensor(f"gu_sh{i}", [(b - a), 262144], BF16, kind="Internal").ap()
             for i, (a, b) in enumerate(GU_SPLITS)]
    gu_all = [nc.dram_tensor(f"gu_all{i}", [8 * (b - a), 262144], BF16,
                             kind="Internal", addr_space="Shared").ap()
              for i, (a, b) in enumerate(GU_SPLITS)]
    w_lin_all = nc.dram_tensor("w_lin_all", [64, 262144], BF16, kind="Internal",
                               addr_space="Shared").ap()

    def w_lin_row(w, mc):
        widx = {"wqt": 0, "wkt": 1, "wvt": 2, "wot": 3}[w]
        return (mc // 2) * 8 + widx * 2 + (mc % 2)
    dn_sh = nc.dram_tensor("dn_sh", [6, 262144], BF16, kind="Internal").ap()
    dn_all = nc.dram_tensor("dn_all", [48, 262144], BF16, kind="Internal",
                            addr_space="Shared").ap()
    kv_cci = nc.dram_tensor("kv_cci", [2 * 2048, 64], F32, kind="Internal").ap()
    kv_cco = nc.dram_tensor("kv_cco", [2 * 2048, 64], F32, kind="Internal").ap()
    ks_cci = nc.dram_tensor("ks_cci", [2 * 128, 16], F32, kind="Internal").ap()
    ks_cco = nc.dram_tensor("ks_cco", [2 * 128, 16], F32, kind="Internal").ap()

    ctx = Ctx(nc)
    ar = Arena(nc, 207360)

    x_tm = [ar.alloc(D, F32) for _ in range(NT)]
    xb = [Buf(f"x{j}") for j in range(NT)]
    ident = ar.alloc(128, BF16)
    ident_b = Buf("ident")
    ones_row = ar.alloc(128, BF16)
    ones_b = Buf("ones")
    bo_bf = ar.alloc(D, BF16)
    bo_bb = Buf("bo_bf")
    selb = ar.alloc(2, F32)
    selb_b = Buf("selb")
    selb_bf = ar.alloc(2, BF16)
    selb_bfb = Buf("selb_bf")
    gus_sc = ar.alloc(16 * 11, F32)
    gus_b = Buf("gus_sc")
    dns_sc = ar.alloc(16 * 6, F32)
    dns_b = Buf("dns_sc")
    trash = ar.alloc(1024, BF16)
    trash_b = Buf("trash")
    statc = ar.alloc(16, F32)
    stat_b = Buf("stat")

    psum = nc.alloc_psum_tensor("psum", [128, 4096], F32).ap()
    pbank = [Buf(f"pb{k}") for k in range(8)]

    def pbf(bank):
        return psum[:, bank * 512:(bank + 1) * 512].bitcast(BF16)

    def coll(fn, reads=(), writes=()):
        lane = ctx.qrr["qcc"]
        ctx.qrr["qcc"] = (lane + 1) % 2
        sem = f"qcc{lane}"
        prev = ctx.counts[sem]
        return ctx.op("gp", fn, reads=reads, writes=writes, sem=sem, inc=1,
                      raw_deps=[(sem, prev)])

    dump_outs = {}

    def dump(name, ap, reads):
        if name not in dumps:
            return
        shp = list(ap.shape)
        d = nc.dram_tensor("dump_" + name, shp, ap.dtype, kind="ExternalOutput").ap()
        dump_outs[name] = d
        dma("gp", d, ap, reads=reads, writes=(Buf("dmp_" + name),))

    def dma(stream, out, in_, reads=(), writes=(), raw_deps=(), transpose=False):
        q = {"sy": "qsy", "gp": "qgp", "sc": "qsc"}[stream]
        lane = ctx.qrr[q]
        ctx.qrr[q] = (lane + 1) % 8
        sem = f"{q}{lane}"
        prev = ctx.counts[sem]
        if transpose:
            fn = lambda e, o=out, i=in_: e.dma_start_transpose(out=o, in_=i)
        else:
            fn = lambda e, o=out, i=in_: e.dma_start(out=o, in_=i)
        return ctx.op(stream, fn, reads=reads, writes=writes, sem=sem, inc=16,
                      raw_deps=list(raw_deps) + [(sem, prev)])

    def vop(fn, reads=(), writes=(), stream="dv"):
        return ctx.op(stream, fn, reads=reads, writes=writes,
                      sem=ctx.eng_sem[stream], inc=1)

    def emit_ln(j, dst_tm, dst_tm_b, dst_fm, dst_fm_b, col, stg, stg_b):
        mcol = statc[:, 0:1]
        rcol = statc[:, 1:2]
        sqa = statc[:, 2:3]
        sqb = statc[:, 3:4]
        vop(lambda e, j=j, mcol=mcol: e.tensor_reduce(
            out=mcol, in_=x_tm[j], axis=AX.X, op=ALU.add),
            reads=(xb[j],), writes=(stat_b,))
        vop(lambda e, mcol=mcol: e.tensor_scalar(out=mcol, in0=mcol,
            scalar1=-1.0 / D, scalar2=None, op0=ALU.mult),
            reads=(stat_b,), writes=(stat_b,))
        ctx.op("sc", lambda e, j=j, mcol=mcol, sqa=sqa: e.activation(
            out=trash, in_=x_tm[j][:, 0:1024], func=AF.Square,
            bias=mcol, scale=1.0, accum_out=sqa),
            reads=(xb[j], stat_b), writes=(trash_b, stat_b), sem="sc", inc=1)
        ctx.op("sc", lambda e, j=j, mcol=mcol, sqb=sqb: e.activation(
            out=trash, in_=x_tm[j][:, 1024:2048], func=AF.Square,
            bias=mcol, scale=1.0, accum_out=sqb),
            reads=(xb[j], stat_b), writes=(trash_b, stat_b), sem="sc", inc=1)
        vop(lambda e, sqa=sqa, sqb=sqb: e.tensor_tensor(out=sqa, in0=sqa, in1=sqb,
            op=ALU.add), reads=(stat_b,), writes=(stat_b,))
        vop(lambda e, sqa=sqa: e.tensor_scalar(out=sqa, in0=sqa, scalar1=1.0 / D,
            scalar2=EPS_LN, op0=ALU.mult, op1=ALU.add),
            reads=(stat_b,), writes=(stat_b,))
        ctx.op("sc", lambda e, sqa=sqa, rcol=rcol: e.activation(
            out=rcol, in_=sqa, func=AF.Sqrt),
            reads=(stat_b,), writes=(stat_b,), sem="sc", inc=1)
        vop(lambda e, rcol=rcol: e.reciprocal(out=rcol, in_=rcol),
            reads=(stat_b,), writes=(stat_b,))
        vop(lambda e, j=j, mcol=mcol, rcol=rcol, dst_tm=dst_tm: e.tensor_scalar(
            out=dst_tm, in0=x_tm[j], scalar1=mcol, scalar2=rcol,
            op0=ALU.add, op1=ALU.mult),
            reads=(xb[j], stat_b), writes=(dst_tm_b,))
        for k4 in range(0, KC, 4):
            for t in range(4):
                kc = k4 + t
                ctx.op("pe", lambda e, kc=kc, t=t, dst_tm=dst_tm: e.transpose(
                    pbf(6)[:, t * 128:(t + 1) * 128],
                    dst_tm[:, kc * 128:(kc + 1) * 128], ident),
                    reads=(dst_tm_b, ident_b), writes=(pbank[6],), sem="pe", inc=1)
            vop(lambda e, stg=stg: e.tensor_copy(out=stg, in_=pbf(6)[:, 0:512]),
                reads=(pbank[6],), writes=(stg_b,))
            for t in range(4):
                kc = k4 + t
                vop(lambda e, kc=kc, t=t, col=col, dst_fm=dst_fm, stg=stg:
                    e.tensor_copy(out=dst_fm[kc][:, col:col + 128],
                                  in_=stg[:, t * 128:(t + 1) * 128]),
                    reads=(stg_b,), writes=(dst_fm_b[kc],))

    for j in range(NT):
        dma("gp", x_tm[j], x_in[j * 128:(j + 1) * 128, :], writes=(xb[j],))
    ctx.op("gp", lambda e: e.memset(ident, 0.0), writes=(ident_b,), sem="gp", inc=1)
    ctx.op("gp", lambda e: e.affine_select(out=ident, in_=ident, compare_op=ALU.not_equal,
                                           fill=1.0, base=0, pattern=[[-1, 128]],
                                           channel_multiplier=1),
           reads=(ident_b,), writes=(ident_b,), sem="gp", inc=1)
    ctx.op("gp", lambda e: e.memset(ones_row, 1.0), writes=(ones_b,), sem="gp", inc=1)
    dma("gp", selb[0:1, :], selb_in[:, :], writes=(selb_b,))
    vop(lambda e: e.tensor_copy(out=selb_bf[0:1, :], in_=selb[0:1, :]),
        reads=(selb_b,), writes=(selb_bfb,))

    ar.push()
    bo_f = ar.alloc(D, F32)
    bo_fb = Buf("bo_f")
    dma("gp", bo_f[0:1, :], bo_in[:, :], writes=(bo_fb,))
    vop(lambda e: e.tensor_copy(out=bo_bf[0:1, :], in_=bo_f[0:1, :]),
        reads=(bo_fb,), writes=(bo_bb,))
    sc_f16 = ar.alloc(16 * 11, F16)
    sc_b = Buf("sc_f16")
    dma("gp", sc_f16.rearrange("p (j g) -> p j g", g=16),
        gus_in.rearrange("(j p) g -> p j g", p=128), writes=(sc_b,))
    vop(lambda e: e.tensor_copy(out=gus_sc, in_=sc_f16), reads=(sc_b,), writes=(gus_b,))
    scd_f16 = ar.alloc(16 * 6, F16)
    scd_b = Buf("scd_f16")
    dma("gp", scd_f16.rearrange("p (j g) -> p j g", g=6),
        dns_in.rearrange("(j p) g -> p j g", p=128), writes=(scd_b,))
    vop(lambda e: e.tensor_copy(out=dns_sc, in_=scd_f16), reads=(scd_b,), writes=(dns_b,))

    ar.pop()

    dn_lin_b = Buf("dn_lin")
    attn_lin_b = Buf("attn_lin")
    wshb = {w: Buf(f"{w}shb") for w in ("wqt", "wkt", "wvt", "wot")}

    def conv_jobs(cva):
        wcv_f, wcv_fb, wcv_bf, wcv_bb = cva
        jobs = []
        for wname, win in (("wqt", wq_in), ("wkt", wk_in), ("wvt", wv_in)):
            for s in range(2):
                def cj(wname=wname, win=win, s=s):
                    dma("sc", wcv_f.rearrange("p (k m) -> p k m", m=128),
                        win.rearrange("(k p) m -> p k m", p=128)
                        [:, :, s * 128:(s + 1) * 128], writes=(wcv_fb,))
                    vop(lambda e, wcv_bf=wcv_bf, wcv_f=wcv_f:
                        e.tensor_copy(out=wcv_bf, in_=wcv_f),
                        reads=(wcv_fb,), writes=(wcv_bb,))
                    dma("sc", wsh[wname].rearrange("n (p f) -> n p f", p=128)[s],
                        wcv_bf, reads=(wcv_bb,), writes=(wshb[wname],))
                jobs.append(cj)
        for s in range(2):
            def cjo(s=s):
                dma("sc", wcv_f, wo_in[s * 128:(s + 1) * 128, :], writes=(wcv_fb,))
                vop(lambda e, wcv_bf=wcv_bf, wcv_f=wcv_f:
                    e.tensor_copy(out=wcv_bf, in_=wcv_f),
                    reads=(wcv_fb,), writes=(wcv_bb,))
                dma("sc", wsh["wot"].rearrange("n (p f) -> n p f", p=128)[s], wcv_bf,
                    reads=(wcv_bb,), writes=(wshb["wot"],))
            jobs.append(cjo)
        jobs.append(lambda: coll(lambda e: e.collective_compute(
            "AllGather", ALU.bypass, replica_groups=[list(range(NCORE))],
            ins=[wsh_all.tensor.ap().opt()], outs=[w_lin_all.tensor.ap().opt()]),
            reads=tuple(wshb.values()), writes=(attn_lin_b,)))
        return jobs

    def emit_dequant_dve(packed_ap, row_off, n_i32, scale_cols, scale_b,
                         scale_col_off, dqa, pb):
        w_t = dqa["w"]
        nib = dqa["nib"]
        lo3 = dqa["lo3"]
        sgn = dqa["sgn"]
        bits = dqa["bits"]
        lo3f = dqa["lo3f"]
        corr = dqa["corr"]
        dq_bf = dqa["dqbf"][pb]
        dqbfb = dqa["dqbfb"][pb]
        dma("sy", w_t[:, 0:n_i32], packed_ap[row_off:row_off + 128, 0:n_i32],
            writes=(dqa["wb"],))
        n_sub = (n_i32 + DQ_SUB_I32 - 1) // DQ_SUB_I32
        for s in range(n_sub):
            c0 = s * DQ_SUB_I32
            c1 = min(n_i32, c0 + DQ_SUB_I32)
            nw = (c1 - c0) * 8
            o0 = c0 * 8
            nibv = nib[:, 0:nw].rearrange("p (g e) -> p g e", e=8)
            for jn in range(8):
                vop(lambda e, jn=jn, c0=c0, c1=c1, nibv=nibv:
                    e.tensor_scalar(out=nibv[:, :, jn], in0=w_t[:, c0:c1],
                                    scalar1=4 * jn, scalar2=15,
                                    op0=ALU.logical_shift_right, op1=ALU.bitwise_and),
                    reads=(dqa["wb"],), writes=(dqa["nibb"],))
            vop(lambda e, nw=nw: e.tensor_scalar(out=lo3[:, 0:nw], in0=nib[:, 0:nw],
                scalar1=7, scalar2=None, op0=ALU.bitwise_and),
                reads=(dqa["nibb"],), writes=(dqa["lo3b"],))
            vop(lambda e, nw=nw: e.tensor_scalar(out=sgn[:, 0:nw], in0=nib[:, 0:nw],
                scalar1=8, scalar2=28, op0=ALU.bitwise_and, op1=ALU.logical_shift_left),
                reads=(dqa["nibb"],), writes=(dqa["sgnb"],))
            vop(lambda e, nw=nw: e.tensor_scalar(out=bits[:, 0:nw], in0=lo3[:, 0:nw],
                scalar1=0xFC, scalar2=(1 << 22), op0=ALU.add, op1=ALU.mult),
                reads=(dqa["lo3b"],), writes=(dqa["bitsb"],))
            vop(lambda e, nw=nw: e.tensor_scalar(out=lo3f[:, 0:nw],
                in0=bits[:, 0:nw].bitcast(F32), scalar1=-1.0, scalar2=0.0,
                op0=ALU.add, op1=ALU.min),
                reads=(dqa["bitsb"],), writes=(dqa["lo3fb"],))
            vop(lambda e, nw=nw: e.tensor_tensor(out=corr[:, 0:nw],
                in0=bits[:, 0:nw].bitcast(F32), in1=lo3f[:, 0:nw], op=ALU.add),
                reads=(dqa["bitsb"], dqa["lo3fb"]), writes=(dqa["corrb"],),
                stream="gp")
            vop(lambda e, nw=nw: e.tensor_tensor(out=lo3f[:, 0:nw].bitcast(I32),
                in0=corr[:, 0:nw].bitcast(I32), in1=sgn[:, 0:nw], op=ALU.bitwise_or),
                reads=(dqa["corrb"], dqa["sgnb"]), writes=(dqa["lo3fb"],))
            g0 = o0 // 128
            for g in range(nw // 128):
                vop(lambda e, g=g, o0=o0, g0=g0, scale_cols=scale_cols:
                    e.tensor_scalar(out=dq_bf[:, o0 + g * 128:o0 + (g + 1) * 128],
                                    in0=lo3f[:, g * 128:(g + 1) * 128],
                                    scalar1=scale_cols[:, scale_col_off + g0 + g:
                                                       scale_col_off + g0 + g + 1],
                                    scalar2=None, op0=ALU.mult),
                    reads=(dqa["lo3fb"], scale_b), writes=(dqbfb,))

    def emit_dequant_tail(n_i32, tile_outs, dqa, pb):
        dq_bf = dqa["dqbf"][pb]
        dqbfb = dqa["dqbfb"][pb]
        stg = dqa["stg"]
        n_kc = n_i32 * 8 // 128
        for kc4 in range(0, n_kc, 4):
            kn = min(4, n_kc - kc4)
            for t in range(kn):
                kc = kc4 + t
                ctx.op("pe", lambda e, kc=kc, t=t, dq_bf=dq_bf: e.transpose(
                    pbf(7)[:, t * 128:(t + 1) * 128],
                    dq_bf[:, kc * 128:(kc + 1) * 128], ident),
                    reads=(dqbfb, ident_b), writes=(pbank[7],), sem="pe", inc=1)
            vop(lambda e, kn=kn, stg=stg: e.copy(
                out=stg[:, 0:kn * 128], in_=pbf(7)[:, 0:kn * 128]),
                reads=(pbank[7],), writes=(dqa["stgb"],), stream="sc")
            contig = kn > 1 and all(tile_outs[kc4 + t] is None for t in range(1, kn))
            if contig:
                dma("sy", tile_outs[kc4], stg[:, 0:kn * 128],
                    reads=(dqa["stgb"],), writes=(dqa["shb"],))
            else:
                for t in range(kn):
                    dma("sy", tile_outs[kc4 + t], stg[:, t * 128:(t + 1) * 128],
                        reads=(dqa["stgb"],), writes=(dqa["shb"],))

    def dq_alloc():
        dqa = {"shb": Buf("dqsh")}
        dqa["w"] = ar.alloc(704, I32)
        dqa["wb"] = Buf("dqw")
        nw = DQ_SUB_I32 * 8
        for nm in ("nib", "lo3", "sgn", "bits"):
            dqa[nm] = ar.alloc(nw, I32)
            dqa[nm + "b"] = Buf(nm)
        dqa["lo3f"] = ar.alloc(nw, F32)
        dqa["lo3fb"] = Buf("lo3f")
        dqa["corr"] = ar.alloc(nw, F32)
        dqa["corrb"] = Buf("corr")
        dqa["dqbf"] = [ar.alloc(IFF, BF16), ar.alloc(IFF, BF16)]
        dqa["dqbfb"] = [Buf("dqbf0"), Buf("dqbf1")]
        dqa["stg"] = ar.alloc(512, BF16)
        dqa["stgb"] = Buf("dqstg")
        return dqa

    gu_gather_b = [Buf(f"gugather{i}") for i in range(len(GU_SPLITS))]

    def dq_jobs(dqa):
        chunks = []
        for i, (a, b) in enumerate(GU_SPLITS):
            for j in range(a, b):
                def gup_dve(pb, i=i, j=j):
                    emit_dequant_dve(gup_in, j * 128, 256, gus_sc, gus_b, j * 16,
                                     dqa, pb)

                def gup_tail(pb, i=i, j=j, a=a):
                    panel = gu_sh[i].rearrange("n (p f) -> n p f", p=128)[j - a]
                    touts = []
                    for kc4 in range(0, 16, 4):
                        touts.append(panel[:, kc4 * 128:(kc4 + 4) * 128])
                        touts += [None, None, None]
                    emit_dequant_tail(256, touts, dqa, pb)
                chunks.append((gup_dve, gup_tail, None))
            chunks[-1] = (chunks[-1][0], chunks[-1][1],
                          lambda i=i: coll(lambda e, i=i: e.collective_compute(
                              "AllGather", ALU.bypass,
                              replica_groups=[list(range(NCORE))],
                              ins=[gu_sh[i].tensor.ap().opt()],
                              outs=[gu_all[i].tensor.ap().opt()]),
                              reads=(dqa["shb"],), writes=(gu_gather_b[i],)))
        dn_panels = dn_sh.rearrange("n (p f) -> n p f", p=128)
        dn_chunks = []
        for fk in range(16):
            def dn_dve(pb, fk=fk):
                emit_dequant_dve(dnp_in, fk * 128, 96, dns_sc, dns_b, fk * 6,
                                 dqa, pb)

            def dn_tail(pb, fk=fk):
                touts = [dn_panels[s][:, fk * 128:(fk + 1) * 128] for s in range(6)]
                emit_dequant_tail(96, touts, dqa, pb)
            fin = None
            if fk == 15:
                def fin():
                    coll(lambda e: e.collective_compute(
                        "AllGather", ALU.bypass, replica_groups=[list(range(NCORE))],
                        ins=[dn_sh.tensor.ap().opt()], outs=[dn_all.tensor.ap().opt()]),
                        reads=(dqa["shb"],), writes=(dn_lin_b,))
            dn_chunks.append((dn_dve, dn_tail, fin))
        head, rest = chunks[:4], chunks[4:]
        merged = list(head)
        di = 0
        for ch in rest:
            for _ in range(3):
                if di < len(dn_chunks):
                    merged.append(dn_chunks[di])
                    di += 1
            merged.append(ch)
        merged.extend(dn_chunks[di:])
        chunks = merged
        jobs = []
        for k in range(len(chunks) + 1):
            def job(k=k):
                if k < len(chunks):
                    chunks[k][0](k % 2)
                if k > 0:
                    chunks[k - 1][1]((k - 1) % 2)
                    if chunks[k - 1][2] is not None:
                        chunks[k - 1][2]()
            jobs.append(job)
        return jobs

    def emit_ffn(segs, dodump=False, with_jobs=False):
        ntt = 4
        TW = ntt * 128
        nb = TW // 512
        ar.push()
        dqa = dq_alloc() if with_jobs else None
        cva = None
        if with_jobs:
            cva = (dqa["dqbf"][0].bitcast(F32)[:, 0:D], dqa["dqbfb"][0],
                   dqa["dqbf"][1][:, 0:D], dqa["dqbfb"][1])
        xn_fm = [ar.alloc(TW, BF16) for _ in range(KC)]
        xnb = [Buf(f"xn{k}") for k in range(KC)]
        h_fm = [ar.alloc(TW, BF16) for _ in range(ICD)]
        hb = [Buf(f"h{i}") for i in range(ICD)]
        xn_tm = ar.alloc(D, BF16)
        xn_tm_b = Buf("xn_tm")
        NWG = 3
        wg_sb = [ar.alloc(D, BF16) for _ in range(NWG)]
        wg_b = [Buf(f"wg{i}") for i in range(NWG)]
        sg_sb = [ar.alloc(TW, F32) for _ in range(2)]
        sg_b = [Buf(f"sg{i}") for i in range(2)]
        stg = ar.alloc(512, BF16)
        stg_b = Buf("tstg")

        for si, tok0 in enumerate(segs):
            seg_jobs = []
            if with_jobs:
                seg_jobs = dq_jobs(dqa) if si == 0 else conv_jobs(cva)
            emit_ffn_seg(si, tok0, seg_jobs, dodump and si == 0 and tok0 == 0,
                         TW, nb, xn_fm, xnb, h_fm, hb, xn_tm, xn_tm_b,
                         NWG, wg_sb, wg_b, sg_sb, sg_b, stg, stg_b)
        ar.pop()

    def emit_ffn_seg(si, tok0, jobs, dodump, TW, nb, xn_fm, xnb, h_fm, hb,
                     xn_tm, xn_tm_b, NWG, wg_sb, wg_b, sg_sb, sg_b, stg, stg_b):
        ntt = TW // 128
        if dodump:
            dump("x0", x_tm[0], (xb[0],))
        for jt in range(ntt):
            emit_ln(tok0 // 128 + jt, xn_tm, xn_tm_b, xn_fm, xnb, jt * 128, stg, stg_b)
            if dodump and jt == 0:
                dump("xn0", xn_tm, (xn_tm_b,))
                dump("st0", statc, (stat_b,))
        if dodump:
            dump("xnf0", xn_fm[0], (xnb[0],))

        ji = 0
        if si == 0 and jobs:
            na, nb_ = GU_SPLITS[0]
            while ji <= nb_ - na:
                jobs[ji]()
                ji += 1
        pair_seq = [(i, c, j) for i, (a, b) in enumerate(GU_SPLITS)
                    for c in range(4) for j in range(a, b)]
        for pi, (i, c, j) in enumerate(pair_seq):
            for _ in range(3):
                if ji < len(jobs):
                    jobs[ji]()
                    ji += 1
            a, b = GU_SPLITS[i]
            oc = c * 11 + j
            for half in range(2):
                cc = c + 4 * half
                buf = (2 * pi + half) % NWG
                r0 = cc * (b - a) + (j - a)
                dma("sy" if half == 0 else "gp", wg_sb[buf],
                    gu_all[i].rearrange("n (p f) -> n p f", p=128)[r0],
                    reads=(gu_gather_b[i],), writes=(wg_b[buf],))
                bank0 = (0 if half == 0 else nb) + (0 if pi % 2 == 0 else 2 * nb)
                pw = [pbank[bank0 + q] for q in range(nb)]
                for kc in range(KC):
                    for q in range(nb):
                        ctx.op("pe", lambda e, kc=kc, q=q, buf=buf, bank0=bank0,
                               wg_sb=wg_sb, xn_fm=xn_fm:
                               e.matmul(psum[:, (bank0 + q) * 512:(bank0 + q + 1) * 512],
                                        wg_sb[buf][:, kc * 128:(kc + 1) * 128],
                                        xn_fm[kc][:, q * 512:(q + 1) * 512],
                                        start=(kc == 0), stop=(kc == KC - 1)),
                               reads=(wg_b[buf], xnb[kc]),
                               writes=tuple(pw) if kc in (0, KC - 1) else (),
                               sem="pe", inc=1 if kc == KC - 1 else 0)
                if half == 0:
                    sbuf = pi % 2
                    ctx.op("sc", lambda e, sbuf=sbuf, bank0=bank0, nb=nb, sg_sb=sg_sb:
                           e.activation(out=sg_sb[sbuf],
                                        in_=psum[:, bank0 * 512:(bank0 + nb) * 512],
                                        func=AF.Silu),
                           reads=tuple(pw), writes=(sg_b[sbuf],), sem="sc", inc=1)
                else:
                    sbuf = pi % 2
                    vop(lambda e, sbuf=sbuf, oc=oc, bank0=bank0, nb=nb, sg_sb=sg_sb,
                        h_fm=h_fm: e.tensor_tensor(
                            out=h_fm[oc], in0=sg_sb[sbuf],
                            in1=psum[:, bank0 * 512:(bank0 + nb) * 512], op=ALU.mult),
                        reads=(sg_b[sbuf],) + tuple(pw), writes=(hb[oc],))

        while ji < len(jobs):
            jobs[ji]()
            ji += 1
        if dodump:
            dump("h0", h_fm[0], (hb[0],))
            dump("h43", h_fm[43], (hb[43],))
        for tp in range(ntt // 2):
            pdn = ([pbank[q] for q in range(4)], [pbank[4 + q] for q in range(4)])
            for ic in range(ICD):
                buf = ic % NWG
                if ic < 24:
                    row = (ic // 6) * 6 + ic % 6
                else:
                    row = 24 + ((ic - 24) // 5) * 6 + (ic - 24) % 5
                dma("sy" if ic % 2 == 0 else "gp", wg_sb[buf],
                    dn_all.rearrange("n (p f) -> n p f", p=128)[row],
                    reads=(dn_lin_b,), writes=(wg_b[buf],))
                for half in range(2):
                    jt = tp * 2 + half
                    for q in range(4):
                        last = (half == 1 and q == 3)
                        ctx.op("pe", lambda e, ic=ic, buf=buf, half=half, jt=jt, q=q,
                               wg_sb=wg_sb, h_fm=h_fm:
                               e.matmul(psum[:, (half * 4 + q) * 512:
                                              (half * 4 + q + 1) * 512],
                                        h_fm[ic][:, jt * 128:(jt + 1) * 128],
                                        wg_sb[buf][:, q * 512:(q + 1) * 512],
                                        start=(ic == 0), stop=(ic == ICD - 1)),
                               reads=(wg_b[buf], hb[ic]),
                               writes=(tuple(pdn[0]) + tuple(pdn[1]))
                               if ((ic == 0 and half == 0 and q == 0) or
                                   (ic == ICD - 1 and last)) else (),
                               sem="pe", inc=1 if last else 0)
            for half in range(2):
                jt = tp * 2 + half
                j = tok0 // 128 + jt
                pw = pdn[half]
                sqa = statc[:, 4:5]
                sqb = statc[:, 5:6]
                rr = statc[:, 6:7]
                ctx.op("sc", lambda e, half=half, sqa=sqa: e.activation(
                    out=trash, in_=psum[:, half * 2048:half * 2048 + 1024],
                    func=AF.Square, accum_out=sqa),
                    reads=tuple(pw) + (trash_b,), writes=(trash_b, stat_b),
                    sem="sc", inc=1)
                ctx.op("sc", lambda e, half=half, sqb=sqb: e.activation(
                    out=trash, in_=psum[:, half * 2048 + 1024:half * 2048 + 2048],
                    func=AF.Square, accum_out=sqb),
                    reads=tuple(pw) + (trash_b,), writes=(trash_b, stat_b),
                    sem="sc", inc=1)
                vop(lambda e, sqa=sqa, sqb=sqb: e.tensor_tensor(out=sqa, in0=sqa,
                    in1=sqb, op=ALU.add), reads=(stat_b,), writes=(stat_b,))
                vop(lambda e, sqa=sqa: e.tensor_scalar(out=sqa, in0=sqa,
                    scalar1=1.0 / D, scalar2=EPS_RMS, op0=ALU.mult, op1=ALU.add),
                    reads=(stat_b,), writes=(stat_b,))
                ctx.op("sc", lambda e, sqa=sqa, rr=rr: e.activation(
                    out=rr, in_=sqa, func=AF.Sqrt),
                    reads=(stat_b,), writes=(stat_b,), sem="sc", inc=1)
                vop(lambda e, rr=rr: e.reciprocal(out=rr, in_=rr),
                    reads=(stat_b,), writes=(stat_b,))
                vop(lambda e, rr=rr: e.tensor_scalar(out=rr, in0=rr, scalar1=0.5,
                    scalar2=None, op0=ALU.mult), reads=(stat_b,), writes=(stat_b,))
                vop(lambda e, half=half, j=j, rr=rr: e.scalar_tensor_tensor(
                    out=x_tm[j], in0=psum[:, half * 2048:(half + 1) * 2048],
                    scalar=rr, in1=x_tm[j], op0=ALU.mult, op1=ALU.add),
                    reads=(stat_b, xb[j]) + tuple(pw), writes=(xb[j],))

    def emit_attn():
        ar.push()
        qkvn_fm = [ar.alloc(T, BF16) for _ in range(KC)]
        qkvn_fmb = [Buf(f"qkvnf{k}") for k in range(KC)]
        kvbd = [ar.alloc(128, BF16) for _ in range(NH2)]
        kvbd_b = [Buf(f"kvbd{c}") for c in range(NH2)]
        kscols = ar.alloc(32, BF16)
        kscols_b = Buf("kscols")
        qkvn = ar.alloc(D, BF16)
        qkvn_b = Buf("qkvn")
        rn = ar.alloc(32, F32)
        rn_b = Buf("rn")
        selc = ar.alloc(2, F32)
        selc_b = Buf("selc")

        ar.push()
        xn2_fm = [ar.alloc(T, BF16) for _ in range(KC)]
        xn2b = [Buf(f"xn2{k}") for k in range(KC)]
        ar.push()
        xn_tm2 = ar.alloc(D, BF16)
        xn_tm2_b = Buf("xn_tm2")
        stg2 = ar.alloc(512, BF16)
        stg2_b = Buf("stg2")
        for j in range(NT):
            emit_ln(j, xn_tm2, xn_tm2_b, xn2_fm, xn2b, j * 128, stg2, stg2_b)
        ar.pop()
        ctx.barrier()

        q_fm = [ar.alloc(T, BF16) for _ in range(KC)]
        qfb = [Buf(f"qf{m}") for m in range(KC)]
        kv_ws = [ar.alloc(D, BF16) for _ in range(2)]
        kv_wsb = [Buf(f"kvws{i}") for i in range(2)]

        def wtile_dma(wname, mc, buf):
            dma("sy", kv_ws[buf],
                w_lin_all.rearrange("n (p f) -> n p f", p=128)[w_lin_row(wname, mc)],
                reads=(attn_lin_b,), writes=(kv_wsb[buf],))

        ctx.op("pe", lambda e: e.matmul(psum[:, 7 * 512:7 * 512 + 2],
                                        ones_row[0:1, :], selb_bf[0:1, :],
                                        start=True, stop=True),
               reads=(ones_b, selb_bfb), writes=(pbank[7],), sem="pe", inc=1)
        vop(lambda e: e.tensor_copy(out=selc, in_=psum[:, 7 * 512:7 * 512 + 2]),
            reads=(pbank[7],), writes=(selc_b,))

        ar.push()
        kc_sb = [ar.alloc(T, BF16) for _ in range(2)]
        kc_b = [Buf(f"kc{i}") for i in range(2)]
        vc_sb = [ar.alloc(T, BF16) for _ in range(2)]
        vc_b = [Buf(f"vc{i}") for i in range(2)]
        ks_sb = ar.alloc(16, F32)
        ks_b = Buf("ks")
        kvps = [pbank[4 + q] for q in range(4)]
        NTMP = 4
        ktmp = [ar.alloc(128, BF16) for _ in range(NTMP)]
        ktmp_b = [Buf(f"ktmp{i}") for i in range(NTMP)]
        vtmp = [ar.alloc(128, BF16) for _ in range(NTMP)]
        vtmp_b = [Buf(f"vtmp{i}") for i in range(NTMP)]

        def kv_prod(c16):
            pr = c16 % 2
            kb0 = 0 if pr == 0 else 2
            vb0 = 2 if pr == 0 else 0
            for wi, (wname, dst, dstb, actf, bank0) in enumerate(
                    (("wkt", kc_sb[pr], kc_b[pr], AF.Relu, kb0),
                     ("wvt", vc_sb[pr], vc_b[pr], AF.Identity, vb0))):
                buf = (2 * c16 + wi) % 2
                wtile_dma(wname, c16, buf)
                pw = [pbank[bank0], pbank[bank0 + 1]]
                for kc in range(KC):
                    for q in range(2):
                        ctx.op("pe", lambda e, kc=kc, q=q, buf=buf, bank0=bank0,
                               xn2_fm=xn2_fm:
                               e.matmul(psum[:, (bank0 + q) * 512:(bank0 + q + 1) * 512],
                                        kv_ws[buf][:, kc * 128:(kc + 1) * 128],
                                        xn2_fm[kc][:, q * 512:(q + 1) * 512],
                                        start=(kc == 0), stop=(kc == KC - 1)),
                               reads=(kv_wsb[buf], xn2b[kc]),
                               writes=tuple(pw) if kc in (0, KC - 1) else (),
                               sem="pe", inc=1 if kc == KC - 1 else 0)
                ctx.op("sc", lambda e, dst=dst, actf=actf, bank0=bank0: e.activation(
                    out=dst, in_=psum[:, bank0 * 512:(bank0 + 2) * 512], func=actf),
                    reads=tuple(pw), writes=(dstb,), sem="sc", inc=1)
            vop(lambda e, pr=pr, c16=c16: e.tensor_reduce(
                out=ks_sb[:, c16:c16 + 1], in_=kc_sb[pr], axis=AX.X, op=ALU.add),
                reads=(kc_b[pr],), writes=(ks_b,))

        def kv_tp(c16, jt):
            pr = c16 % 2
            kb0 = 0 if pr == 0 else 2
            tb = (c16 * NT + jt) % NTMP
            off = (tb % 2) * 256
            ctx.op("pe", lambda e, pr=pr, jt=jt, kb0=kb0, off=off: e.transpose(
                pbf(kb0)[:, off:off + 128],
                kc_sb[pr][:, jt * 128:(jt + 1) * 128], ident),
                reads=(kc_b[pr], ident_b), writes=(pbank[kb0],), sem="pe", inc=1)
            ctx.op("pe", lambda e, pr=pr, jt=jt, kb0=kb0, off=off: e.transpose(
                pbf(kb0)[:, off + 128:off + 256],
                vc_sb[pr][:, jt * 128:(jt + 1) * 128], ident),
                reads=(vc_b[pr], ident_b), writes=(pbank[kb0],), sem="pe", inc=1)
            vop(lambda e, tb=tb, kb0=kb0, off=off: e.tensor_copy(
                out=ktmp[tb], in_=pbf(kb0)[:, off:off + 128]),
                reads=(pbank[kb0],), writes=(ktmp_b[tb],))
            vop(lambda e, tb=tb, kb0=kb0, off=off: e.tensor_copy(
                out=vtmp[tb], in_=pbf(kb0)[:, off + 128:off + 256]),
                reads=(pbank[kb0],), writes=(vtmp_b[tb],))

        def kv_mm(c16, jt):
            tb = (c16 * NT + jt) % NTMP
            ctx.op("pe", lambda e, c16=c16, jt=jt, tb=tb: e.matmul(
                psum[:, 2048 + c16 * 128:2048 + (c16 + 1) * 128],
                ktmp[tb], vtmp[tb], start=(jt == 0), stop=(jt == NT - 1)),
                reads=(ktmp_b[tb], vtmp_b[tb]),
                writes=tuple(kvps) if (c16 == 0 and jt == 0) or
                (c16 == NH2 - 1 and jt == NT - 1) else (),
                sem="pe", inc=1)

        steps = []
        for c16 in range(NH2):
            steps.append(("prod", c16, None))
            for jt in range(NT):
                steps.append(("tp", c16, jt))
        mmq = []
        for kind, c16, jt in steps:
            if kind == "prod":
                kv_prod(c16)
            else:
                kv_tp(c16, jt)
                mmq.append((c16, jt))
                if len(mmq) > 2:
                    kv_mm(*mmq.pop(0))
        while mmq:
            kv_mm(*mmq.pop(0))

        kvpart = ar.alloc(NH2 * 64, F32)
        kvp_b = Buf("kvpart")
        kvsc = ar.alloc(NH2 * 64, F32)
        kvsc_b = Buf("kvsc")
        kssc = ar.alloc(16, F32)
        kssc_b = Buf("kssc")
        for c16 in range(NH2):
            vop(lambda e, c16=c16: e.tensor_copy(
                out=kvpart[0:64, c16 * 64:(c16 + 1) * 64],
                in_=psum[0:64, 2048 + c16 * 128:2048 + c16 * 128 + 64]),
                reads=(kvps[c16 // 4],), writes=(kvp_b,))
            vop(lambda e, c16=c16: e.tensor_copy(
                out=kvpart[64:128, c16 * 64:(c16 + 1) * 64],
                in_=psum[64:128, 2048 + c16 * 128 + 64:2048 + c16 * 128 + 128]),
                reads=(kvps[c16 // 4],), writes=(kvp_b,))
        kvcci_b = Buf("kvcci")
        kscci_b = Buf("kscci")
        for b in range(2):
            vop(lambda e, b=b: e.tensor_scalar(out=kvsc, in0=kvpart,
                scalar1=selc[:, b:b + 1], scalar2=None, op0=ALU.mult),
                reads=(kvp_b, selc_b), writes=(kvsc_b,))
            vop(lambda e, b=b: e.tensor_scalar(out=kssc, in0=ks_sb,
                scalar1=selc[:, b:b + 1], scalar2=None, op0=ALU.mult),
                reads=(ks_b, selc_b), writes=(kssc_b,))
            dma("gp", kv_cci[b * 2048:(b + 1) * 2048, :].rearrange(
                "(c p) e -> p c e", p=128),
                kvsc.rearrange("p (c e) -> p c e", e=64),
                reads=(kvsc_b,), writes=(kvcci_b,))
            dma("gp", ks_cci[b * 128:(b + 1) * 128, :], kssc,
                reads=(kssc_b,), writes=(kscci_b,))
        cc_b = Buf("kvar")
        coll(lambda e: e.collective_compute(
            "AllReduce", ALU.add, replica_groups=[list(range(NCORE))],
            ins=[kv_cci.tensor.ap().opt()], outs=[kv_cco.tensor.ap().opt()]),
            reads=(kvcci_b,), writes=(cc_b,))
        coll(lambda e: e.collective_compute(
            "AllReduce", ALU.add, replica_groups=[list(range(NCORE))],
            ins=[ks_cci.tensor.ap().opt()], outs=[ks_cco.tensor.ap().opt()]),
            reads=(kscci_b,), writes=(cc_b,))

        for mc in range(KC):
            buf = mc % 2
            wtile_dma("wqt", mc, buf)
            bank0 = 0 if mc % 2 == 0 else 2
            pw = [pbank[bank0], pbank[bank0 + 1]]
            for kc in range(KC):
                for q in range(2):
                    ctx.op("pe", lambda e, kc=kc, q=q, buf=buf, bank0=bank0,
                           xn2_fm=xn2_fm:
                           e.matmul(psum[:, (bank0 + q) * 512:(bank0 + q + 1) * 512],
                                    kv_ws[buf][:, kc * 128:(kc + 1) * 128],
                                    xn2_fm[kc][:, q * 512:(q + 1) * 512],
                                    start=(kc == 0), stop=(kc == KC - 1)),
                           reads=(kv_wsb[buf], xn2b[kc]),
                           writes=tuple(pw) if kc in (0, KC - 1) else (),
                           sem="pe", inc=1 if kc == KC - 1 else 0)
            ctx.op("sc", lambda e, mc=mc, bank0=bank0, q_fm=q_fm: e.activation(
                out=q_fm[mc], in_=psum[:, bank0 * 512:(bank0 + 2) * 512], func=AF.Relu),
                reads=tuple(pw), writes=(qfb[mc],), sem="sc", inc=1)

        kvld = [kvpart, kvsc]
        kvld_b = [kvp_b, kvsc_b]
        ksld = [ks_sb, kssc]
        ksld_b = [ks_b, kssc_b]
        for b in range(2):
            dma("gp", kvld[b].rearrange("p (c e) -> p c e", e=64),
                kv_cco[b * 2048:(b + 1) * 2048, :].rearrange("(c p) e -> p c e", p=128),
                reads=(cc_b,), writes=(kvld_b[b],))
            dma("gp", ksld[b], ks_cco[b * 128:(b + 1) * 128, :],
                reads=(cc_b,), writes=(ksld_b[b],))
        vop(lambda e: e.tensor_scalar(out=kvld[0], in0=kvld[0], scalar1=selc[:, 0:1],
            scalar2=None, op0=ALU.mult), reads=(kvld_b[0], selc_b), writes=(kvld_b[0],))
        vop(lambda e: e.scalar_tensor_tensor(out=kvld[0], in0=kvld[1],
            scalar=selc[:, 1:2], in1=kvld[0], op0=ALU.mult, op1=ALU.add),
            reads=(kvld_b[1], kvld_b[0], selc_b), writes=(kvld_b[0],))
        vop(lambda e: e.tensor_scalar(out=ksld[0], in0=ksld[0], scalar1=selc[:, 0:1],
            scalar2=None, op0=ALU.mult), reads=(ksld_b[0], selc_b), writes=(ksld_b[0],))
        vop(lambda e: e.scalar_tensor_tensor(out=ksld[0], in0=ksld[1],
            scalar=selc[:, 1:2], in1=ksld[0], op0=ALU.mult, op1=ALU.add),
            reads=(ksld_b[1], ksld_b[0], selc_b), writes=(ksld_b[0],))
        ctx.op("gp", lambda e: e.memset(kscols, 0.0), writes=(kscols_b,),
               sem="gp", inc=1)
        for c16 in range(NH2):
            ctx.op("gp", lambda e, c16=c16: e.memset(kvbd[c16], 0.0),
                   writes=(kvbd_b[c16],), sem="gp", inc=1)
            vop(lambda e, c16=c16: e.tensor_copy(
                out=kvbd[c16][0:64, 0:64], in_=kvld[0][0:64, c16 * 64:(c16 + 1) * 64]),
                reads=(kvld_b[0],), writes=(kvbd_b[c16],))
            vop(lambda e, c16=c16: e.tensor_copy(
                out=kvbd[c16][64:128, 64:128],
                in_=kvld[0][64:128, c16 * 64:(c16 + 1) * 64]),
                reads=(kvld_b[0],), writes=(kvbd_b[c16],))
            vop(lambda e, c16=c16: e.tensor_copy(
                out=kscols[0:64, 2 * c16:2 * c16 + 1], in_=ksld[0][0:64, c16:c16 + 1]),
                reads=(ksld_b[0],), writes=(kscols_b,))
            vop(lambda e, c16=c16: e.tensor_copy(
                out=kscols[64:128, 2 * c16 + 1:2 * c16 + 2],
                in_=ksld[0][64:128, c16:c16 + 1]),
                reads=(ksld_b[0],), writes=(kscols_b,))
        ar.pop()
        ctx.barrier()

        rn_all = ar.alloc(256, F32)
        rn_all_b = Buf("rn_all")
        for jt in range(NT):
            for c16 in range(NH2):
                first = (jt == 0 and c16 == 0)
                last = (jt == NT - 1 and c16 == NH2 - 1)
                ctx.op("pe", lambda e, c16=c16, jt=jt, q_fm=q_fm: e.matmul(
                    psum[:, 7 * 512 + jt * 32 + 2 * c16:
                         7 * 512 + jt * 32 + 2 * c16 + 2],
                    q_fm[c16][:, jt * 128:(jt + 1) * 128],
                    kscols[:, 2 * c16:2 * c16 + 2], start=True, stop=True),
                    reads=(qfb[c16], kscols_b),
                    writes=(pbank[7],) if (first or last) else (),
                    sem="pe", inc=1 if last else 0)
        vop(lambda e: e.tensor_scalar(out=rn_all, in0=psum[:, 7 * 512:7 * 512 + 256],
            scalar1=CLIP, scalar2=None, op0=ALU.max),
            reads=(pbank[7],), writes=(rn_all_b,))
        vop(lambda e: e.reciprocal(out=rn_all, in_=rn_all),
            reads=(rn_all_b,), writes=(rn_all_b,))

        qkvn2 = ar.alloc(D, BF16)
        qkvn2_b = Buf("qkvn2")
        qkvns = [(qkvn, qkvn_b), (qkvn2, qkvn2_b)]

        def qkv_mm(jt):
            g0 = 0 if jt % 2 == 0 else 4
            pq = [pbank[g0 + q] for q in range(4)]
            for c16 in range(NH2):
                ctx.op("pe", lambda e, c16=c16, jt=jt, g0=g0, q_fm=q_fm: e.matmul(
                    psum[:, g0 * 512 + c16 * 128:g0 * 512 + (c16 + 1) * 128],
                    q_fm[c16][:, jt * 128:(jt + 1) * 128], kvbd[c16],
                    start=True, stop=True),
                    reads=(qfb[c16], kvbd_b[c16]),
                    writes=tuple(pq) if c16 in (0, NH2 - 1) else (),
                    sem="pe", inc=1 if c16 == NH2 - 1 else 0)
            qn, qnb = qkvns[jt % 2]
            for h in range(32):
                vop(lambda e, h=h, g0=g0, jt=jt, qn=qn: e.tensor_scalar(
                    out=qn[:, h * 64:(h + 1) * 64],
                    in0=psum[:, g0 * 512 + h * 64:g0 * 512 + (h + 1) * 64],
                    scalar1=rn_all[:, jt * 32 + h:jt * 32 + h + 1],
                    scalar2=None, op0=ALU.mult),
                    reads=(rn_all_b,) + tuple(pq), writes=(qnb,))

        def qkv_tr(jt):
            g0 = 0 if jt % 2 == 0 else 4
            qn, qnb = qkvns[jt % 2]
            for k4 in range(0, KC, 4):
                for t in range(4):
                    ec = k4 + t
                    ctx.op("pe", lambda e, ec=ec, t=t, g0=g0, qn=qn: e.transpose(
                        pbf(g0)[:, t * 128:(t + 1) * 128],
                        qn[:, ec * 128:(ec + 1) * 128], ident),
                        reads=(qnb, ident_b), writes=(pbank[g0],), sem="pe", inc=1)
                for t in range(4):
                    ec = k4 + t
                    vop(lambda e, ec=ec, t=t, jt=jt, g0=g0, qkvn_fm=qkvn_fm:
                        e.tensor_copy(
                            out=qkvn_fm[ec][:, jt * 128:(jt + 1) * 128],
                            in_=pbf(g0)[:, t * 128:(t + 1) * 128]),
                        reads=(pbank[g0],), writes=(qkvn_fmb[ec],))

        qkv_mm(0)
        for jt in range(1, NT):
            qkv_mm(jt)
            qkv_tr(jt - 1)
        qkv_tr(NT - 1)
        ar.pop()
        ctx.barrier()

        ar.push()
        wo_sb = [ar.alloc(D, BF16) for _ in range(KC)]
        wo_b = [Buf(f"wo{e}") for e in range(KC)]
        for ec in range(KC):
            dma("sy", wo_sb[ec],
                w_lin_all.rearrange("n (p f) -> n p f", p=128)[w_lin_row("wot", ec)],
                reads=(attn_lin_b,), writes=(wo_b[ec],))
        for jt in range(NT):
            g0 = 0 if jt % 2 == 0 else 4
            po = [pbank[g0 + q] for q in range(4)]
            for ec in range(KC):
                for q in range(4):
                    ctx.op("pe", lambda e, ec=ec, q=q, jt=jt, g0=g0, qkvn_fm=qkvn_fm,
                           wo_sb=wo_sb: e.matmul(
                        psum[:, (g0 + q) * 512:(g0 + q + 1) * 512],
                        qkvn_fm[ec][:, jt * 128:(jt + 1) * 128],
                        wo_sb[ec][:, q * 512:(q + 1) * 512],
                        start=(ec == 0), stop=False),
                        reads=(qkvn_fmb[ec], wo_b[ec]),
                        writes=tuple(po) if ec == 0 else (),
                        sem="pe", inc=0)
            for q in range(4):
                ctx.op("pe", lambda e, q=q, g0=g0: e.matmul(
                    psum[:, (g0 + q) * 512:(g0 + q + 1) * 512], ones_row[0:1, :],
                    bo_bf[0:1, q * 512:(q + 1) * 512], start=False, stop=True),
                    reads=(ones_b, bo_bb),
                    writes=tuple(po) if q == 3 else (),
                    sem="pe", inc=1 if q == 3 else 0)
            vop(lambda e, jt=jt, g0=g0: e.tensor_tensor(
                out=x_tm[jt], in0=psum[:, g0 * 512:g0 * 512 + 2048],
                in1=x_tm[jt], op=ALU.add), reads=(xb[jt],) + tuple(po),
                writes=(xb[jt],))
        ar.pop()
        ar.pop()

    ctx.barrier()
    if "ffn1" in phases:
        emit_ffn([0, 512], dodump=True, with_jobs=True)
        ctx.barrier()
    if "attn" in phases:
        emit_attn()
        ctx.barrier()
    if "ffn3" in phases:
        emit_ffn([0, 512])
        ctx.barrier()
    for j in range(NT):
        dma("gp", y_out[j * 128:(j + 1) * 128, :], x_tm[j], reads=(xb[j],),
            writes=(Buf(f"y{j}"),))
    for i in range(8):
        ctx._wait("gp", f"qgp{i}", ctx.counts[f"qgp{i}"])
    ctx.emit()
    return nc


def make_in_maps(L0_attn_hidden_states, L0_attn_Wq, L0_attn_Wk, L0_attn_Wv,
                 L0_attn_Wo, L0_attn_bo, L0_ffn_gate_up_packed,
                 L0_ffn_gate_up_scales, L0_ffn_down_packed, L0_ffn_down_scales):
    x = np.ascontiguousarray(np.asarray(L0_attn_hidden_states, dtype=np.float32)
                             .reshape(NCORE * T, D))
    wq = np.ascontiguousarray(np.asarray(L0_attn_Wq, dtype=np.float32))
    wk = np.ascontiguousarray(np.asarray(L0_attn_Wk, dtype=np.float32))
    wv = np.ascontiguousarray(np.asarray(L0_attn_Wv, dtype=np.float32))
    wo = np.ascontiguousarray(np.asarray(L0_attn_Wo, dtype=np.float32))
    bo = np.ascontiguousarray(np.asarray(L0_attn_bo, dtype=np.float32)).reshape(1, D)
    gup = np.ascontiguousarray(np.asarray(L0_ffn_gate_up_packed, dtype=np.int32))
    gus = np.ascontiguousarray(np.asarray(L0_ffn_gate_up_scales, dtype=np.float16).T)
    dnp = np.ascontiguousarray(np.asarray(L0_ffn_down_packed, dtype=np.int32))
    dns = np.ascontiguousarray(np.asarray(L0_ffn_down_scales, dtype=np.float16).T)
    ic_cnt = [6, 6, 6, 6, 5, 5, 5, 5]
    ic_start = [0, 6, 12, 18, 24, 29, 34, 39]
    in_maps = []
    for c in range(NCORE):
        sel = np.zeros((1, 2), np.float32)
        sel[0, 0 if c < 4 else 1] = 1.0
        s0, cnt = ic_start[c], ic_cnt[c]
        dnp_c = np.zeros((D, 96), np.int32)
        dnp_c[:, 0:cnt * 16] = dnp[:, s0 * 16:(s0 + cnt) * 16]
        dns_c = np.zeros((D, 6), np.float16)
        dns_c[:, 0:cnt] = dns[:, s0:s0 + cnt]
        in_maps.append({
            "x_in": x[c * T:(c + 1) * T],
            "gup": gup[c * 1408:(c + 1) * 1408],
            "gus": gus[c * 1408:(c + 1) * 1408],
            "dnp": np.ascontiguousarray(dnp_c),
            "dns": np.ascontiguousarray(dns_c),
            "wq": np.ascontiguousarray(wq[:, c * 256:(c + 1) * 256]),
            "wk": np.ascontiguousarray(wk[:, c * 256:(c + 1) * 256]),
            "wv": np.ascontiguousarray(wv[:, c * 256:(c + 1) * 256]),
            "wo": np.ascontiguousarray(wo[c * 256:(c + 1) * 256]),
            "bo": bo,
            "selb": sel,
        })
    return in_maps


_NC_CACHE = None


def kernel(L0_attn_hidden_states, L0_attn_Wq, L0_attn_Wk, L0_attn_Wv, L0_attn_Wo,
           L0_attn_bo, L0_ffn_gate_up_packed, L0_ffn_gate_up_scales,
           L0_ffn_down_packed, L0_ffn_down_scales):
    global _NC_CACHE
    in_maps = make_in_maps(L0_attn_hidden_states, L0_attn_Wq, L0_attn_Wk, L0_attn_Wv,
                           L0_attn_Wo, L0_attn_bo, L0_ffn_gate_up_packed,
                           L0_ffn_gate_up_scales, L0_ffn_down_packed,
                           L0_ffn_down_scales)
    if _NC_CACHE is None:
        _NC_CACHE = build_program()
    nc = _NC_CACHE
    res = run_bass_kernel_spmd(nc, in_maps, core_ids=list(range(NCORE)))
    out = np.concatenate([r["y"] for r in res.results], axis=0)
    return out.reshape(2, 4096, 2048)


# version 29
# speedup vs baseline: 1.0213x; 1.0213x over previous
import numpy as np
import concourse.bass as bass
import concourse.bass_utils as _bu
import concourse.mybir as mybir
from concourse.bass_utils import run_bass_kernel_spmd

if not getattr(_bu, "_ldw_patched", False):
    _orig_run_command = _bu.run_command

    def _run_command_ldw(argv, **kwargs):
        argv = ["--enable-ldw-opt=true" if a == "--enable-ldw-opt=false" else a
                for a in argv]
        return _orig_run_command(argv, **kwargs)

    _bu.run_command = _run_command_ldw
    _bu._ldw_patched = True

F32 = mybir.dt.float32
BF16 = mybir.dt.bfloat16
F16 = mybir.dt.float16
I32 = mybir.dt.int32
AF = mybir.ActivationFunctionType
ALU = mybir.AluOpType
AX = mybir.AxisListType

NCORE = 8
D = 2048
KC = D // 128
T = 1024
NT = T // 128
IFF = 5632
OCG = 2 * IFF // 128
ICD = IFF // 128
NH2 = 16
EPS_LN = 1e-5
EPS_RMS = 1e-6
CLIP = 1e-6

DQ_SUB_I32 = 128
GU_SPLITS = [(0, 2), (2, 4), (4, 6), (6, 8), (8, 11)]


class Buf:
    __slots__ = ("name", "wtok", "rtoks")

    def __init__(self, name):
        self.name = name
        self.wtok = None
        self.rtoks = {}


class Ctx:
    def __init__(self, nc):
        self.nc = nc
        self.sems = {}
        self.counts = {}
        names = ["pe", "dv", "sc", "gp", "qcc0", "qcc1"]
        names += [f"qsy{i}" for i in range(8)] + [f"qgp{i}" for i in range(8)]
        names += [f"qsc{i}" for i in range(8)]
        for s in names:
            self.sems[s] = nc.alloc_semaphore(s)
            self.counts[s] = 0
        self.qrr = {"qsy": 0, "qgp": 0, "qsc": 0, "qcc": 0}
        self.streams = {"pe": [], "dv": [], "sc": [], "gp": [], "sy": []}
        self.marks = {e: {} for e in self.streams}
        self.eng_sem = {"pe": "pe", "dv": "dv", "sc": "sc", "gp": "gp", "sy": None}

    def _wait(self, stream, sem, val):
        if val <= 0:
            return
        if self.marks[stream].get(sem, 0) >= val:
            return
        self.marks[stream][sem] = val
        s = self.sems[sem]
        self.streams[stream].append(lambda eng, s=s, val=val: eng.wait_ge(s, val))

    def op(self, stream, fn, reads=(), writes=(), sem=None, inc=0, raw_deps=()):
        need = {}

        def add(tok):
            if tok is None:
                return
            s, v = tok
            if need.get(s, 0) < v:
                need[s] = v

        for b in reads:
            add(b.wtok)
        for b in writes:
            add(b.wtok)
            for s, v in b.rtoks.items():
                add((s, v))
        for tok in raw_deps:
            add(tok)
        for s, v in sorted(need.items()):
            self._wait(stream, s, v)

        tok = None
        if sem is not None and inc:
            self.counts[sem] += inc
            tok = (sem, self.counts[sem])
            semh = self.sems[sem]

            def thunk(eng, fn=fn, semh=semh, cnt=inc):
                fn(eng).then_inc(semh, cnt)
            self.streams[stream].append(thunk)
        else:
            self.streams[stream].append(lambda eng, fn=fn: fn(eng))

        if tok is not None:
            for b in writes:
                b.wtok = tok
                b.rtoks = {}
            for b in reads:
                s, v = tok
                if b.rtoks.get(s, 0) < v:
                    b.rtoks[s] = v
        return tok

    def barrier(self):
        snap = dict(self.counts)
        for stream in self.streams:
            own = self.eng_sem[stream]
            for s, v in snap.items():
                if s != own:
                    self._wait(stream, s, v)

    def emit(self):
        with self.nc.Block() as block:
            block.sync(lambda eng: [t(eng) for t in self.streams["sy"]])
            block.tensor(lambda eng: [t(eng) for t in self.streams["pe"]])
            block.vector(lambda eng: [t(eng) for t in self.streams["dv"]])
            block.scalar(lambda eng: [t(eng) for t in self.streams["sc"]])
            block.gpsimd(lambda eng: [t(eng) for t in self.streams["gp"]])


class Arena:
    def __init__(self, nc, total_bytes):
        self.t = nc.alloc_sbuf_tensor("arena", [128, total_bytes // 4], F32)
        self.base = self.t.ap()
        self.off = 0
        self.total = total_bytes
        self.stack = []
        self.peak = 0

    def push(self):
        self.stack.append(self.off)

    def pop(self):
        self.off = self.stack.pop()

    def alloc(self, cols, dtype, rows=128):
        esz = {F32: 4, I32: 4, BF16: 2, F16: 2}[dtype]
        self.off = (self.off + 3) // 4 * 4
        start = self.off
        self.off += cols * esz
        self.peak = max(self.peak, self.off)
        assert self.off <= self.total, f"arena overflow: {self.off} > {self.total}"
        c0 = start // 4
        if dtype is F32:
            ap = self.base[:, c0:c0 + cols]
        elif dtype is I32:
            ap = self.base[:, c0:c0 + cols].bitcast(I32)
        else:
            n4 = (cols + 1) // 2
            ap = self.base[:, c0:c0 + n4].bitcast(dtype)[:, 0:cols]
        if rows != 128:
            ap = ap[0:rows, :]
        return ap


def build_program(phases=("ffn1", "attn", "ffn3"), dumps=()):
    nc = bass.Bass("TRN2", target_bir_lowering=False, num_devices=NCORE)

    x_in = nc.dram_tensor("x_in", [T, D], F32, kind="ExternalInput").ap()
    gup_in = nc.dram_tensor("gup", [1408, 256], I32, kind="ExternalInput").ap()
    gus_in = nc.dram_tensor("gus", [1408, 16], F16, kind="ExternalInput").ap()
    dnp_in = nc.dram_tensor("dnp", [D, 96], I32, kind="ExternalInput").ap()
    dns_in = nc.dram_tensor("dns", [D, 6], F16, kind="ExternalInput").ap()
    wq_in = nc.dram_tensor("wq", [D, 256], F32, kind="ExternalInput").ap()
    wk_in = nc.dram_tensor("wk", [D, 256], F32, kind="ExternalInput").ap()
    wv_in = nc.dram_tensor("wv", [D, 256], F32, kind="ExternalInput").ap()
    wo_in = nc.dram_tensor("wo", [256, D], F32, kind="ExternalInput").ap()
    bo_in = nc.dram_tensor("bo", [1, D], F32, kind="ExternalInput").ap()
    selb_in = nc.dram_tensor("selb", [1, 2], F32, kind="ExternalInput").ap()
    y_out = nc.dram_tensor("y", [T, D], F32, kind="ExternalOutput").ap()

    wsh_all = nc.dram_tensor("wsh_all", [8, 262144], BF16, kind="Internal").ap()
    W_IDX = {"wqt": 0, "wkt": 1, "wvt": 2, "wot": 3}
    wsh = {w: wsh_all[W_IDX[w] * 2:W_IDX[w] * 2 + 2, :]
           for w in ("wqt", "wkt", "wvt", "wot")}
    gu_sh = [nc.dram_tensor(f"gu_sh{i}", [(b - a), 262144], BF16, kind="Internal").ap()
             for i, (a, b) in enumerate(GU_SPLITS)]
    gu_all = [nc.dram_tensor(f"gu_all{i}", [8 * (b - a), 262144], BF16,
                             kind="Internal", addr_space="Shared").ap()
              for i, (a, b) in enumerate(GU_SPLITS)]
    w_lin_all = nc.dram_tensor("w_lin_all", [64, 262144], BF16, kind="Internal",
                               addr_space="Shared").ap()

    def w_lin_row(w, mc):
        widx = {"wqt": 0, "wkt": 1, "wvt": 2, "wot": 3}[w]
        return (mc // 2) * 8 + widx * 2 + (mc % 2)
    dn_sh = nc.dram_tensor("dn_sh", [6, 262144], BF16, kind="Internal").ap()
    dn_all = nc.dram_tensor("dn_all", [48, 262144], BF16, kind="Internal",
                            addr_space="Shared").ap()
    kv_cci = nc.dram_tensor("kv_cci", [2 * 2048, 64], F32, kind="Internal").ap()
    kv_cco = nc.dram_tensor("kv_cco", [2 * 2048, 64], F32, kind="Internal").ap()
    ks_cci = nc.dram_tensor("ks_cci", [2 * 128, 16], F32, kind="Internal").ap()
    ks_cco = nc.dram_tensor("ks_cco", [2 * 128, 16], F32, kind="Internal").ap()

    ctx = Ctx(nc)
    ar = Arena(nc, 207360)

    x_tm = [ar.alloc(D, F32) for _ in range(NT)]
    xb = [Buf(f"x{j}") for j in range(NT)]
    ident = ar.alloc(128, BF16)
    ident_b = Buf("ident")
    ones_row = ar.alloc(128, BF16)
    ones_b = Buf("ones")
    bo_bf = ar.alloc(D, BF16)
    bo_bb = Buf("bo_bf")
    selb = ar.alloc(2, F32)
    selb_b = Buf("selb")
    selb_bf = ar.alloc(2, BF16)
    selb_bfb = Buf("selb_bf")
    gus_sc = ar.alloc(16 * 11, F32)
    gus_b = Buf("gus_sc")
    dns_sc = ar.alloc(16 * 6, F32)
    dns_b = Buf("dns_sc")
    trash = ar.alloc(1024, BF16)
    trash_b = Buf("trash")
    statc = ar.alloc(16, F32)
    stat_b = Buf("stat")

    psum = nc.alloc_psum_tensor("psum", [128, 4096], F32).ap()
    pbank = [Buf(f"pb{k}") for k in range(8)]

    def pbf(bank):
        return psum[:, bank * 512:(bank + 1) * 512].bitcast(BF16)

    def coll(fn, reads=(), writes=()):
        lane = ctx.qrr["qcc"]
        ctx.qrr["qcc"] = (lane + 1) % 2
        sem = f"qcc{lane}"
        prev = ctx.counts[sem]
        return ctx.op("gp", fn, reads=reads, writes=writes, sem=sem, inc=1,
                      raw_deps=[(sem, prev)])

    dump_outs = {}

    def dump(name, ap, reads):
        if name not in dumps:
            return
        shp = list(ap.shape)
        d = nc.dram_tensor("dump_" + name, shp, ap.dtype, kind="ExternalOutput").ap()
        dump_outs[name] = d
        dma("gp", d, ap, reads=reads, writes=(Buf("dmp_" + name),))

    def dma(stream, out, in_, reads=(), writes=(), raw_deps=(), transpose=False):
        q = {"sy": "qsy", "gp": "qgp", "sc": "qsc"}[stream]
        lane = ctx.qrr[q]
        ctx.qrr[q] = (lane + 1) % 8
        sem = f"{q}{lane}"
        prev = ctx.counts[sem]
        if transpose:
            fn = lambda e, o=out, i=in_: e.dma_start_transpose(out=o, in_=i)
        else:
            fn = lambda e, o=out, i=in_: e.dma_start(out=o, in_=i)
        return ctx.op(stream, fn, reads=reads, writes=writes, sem=sem, inc=16,
                      raw_deps=list(raw_deps) + [(sem, prev)])

    def vop(fn, reads=(), writes=(), stream="dv"):
        return ctx.op(stream, fn, reads=reads, writes=writes,
                      sem=ctx.eng_sem[stream], inc=1)

    def emit_ln(j, dst_tm, dst_tm_b, dst_fm, dst_fm_b, col, stg, stg_b):
        mcol = statc[:, 0:1]
        rcol = statc[:, 1:2]
        sqa = statc[:, 2:3]
        sqb = statc[:, 3:4]
        vop(lambda e, j=j, mcol=mcol: e.tensor_reduce(
            out=mcol, in_=x_tm[j], axis=AX.X, op=ALU.add),
            reads=(xb[j],), writes=(stat_b,))
        vop(lambda e, mcol=mcol: e.tensor_scalar(out=mcol, in0=mcol,
            scalar1=-1.0 / D, scalar2=None, op0=ALU.mult),
            reads=(stat_b,), writes=(stat_b,))
        ctx.op("sc", lambda e, j=j, mcol=mcol, sqa=sqa: e.activation(
            out=trash, in_=x_tm[j][:, 0:1024], func=AF.Square,
            bias=mcol, scale=1.0, accum_out=sqa),
            reads=(xb[j], stat_b), writes=(trash_b, stat_b), sem="sc", inc=1)
        ctx.op("sc", lambda e, j=j, mcol=mcol, sqb=sqb: e.activation(
            out=trash, in_=x_tm[j][:, 1024:2048], func=AF.Square,
            bias=mcol, scale=1.0, accum_out=sqb),
            reads=(xb[j], stat_b), writes=(trash_b, stat_b), sem="sc", inc=1)
        vop(lambda e, sqa=sqa, sqb=sqb: e.tensor_tensor(out=sqa, in0=sqa, in1=sqb,
            op=ALU.add), reads=(stat_b,), writes=(stat_b,))
        vop(lambda e, sqa=sqa: e.tensor_scalar(out=sqa, in0=sqa, scalar1=1.0 / D,
            scalar2=EPS_LN, op0=ALU.mult, op1=ALU.add),
            reads=(stat_b,), writes=(stat_b,))
        ctx.op("sc", lambda e, sqa=sqa, rcol=rcol: e.activation(
            out=rcol, in_=sqa, func=AF.Sqrt),
            reads=(stat_b,), writes=(stat_b,), sem="sc", inc=1)
        vop(lambda e, rcol=rcol: e.reciprocal(out=rcol, in_=rcol),
            reads=(stat_b,), writes=(stat_b,))
        vop(lambda e, j=j, mcol=mcol, rcol=rcol, dst_tm=dst_tm: e.tensor_scalar(
            out=dst_tm, in0=x_tm[j], scalar1=mcol, scalar2=rcol,
            op0=ALU.add, op1=ALU.mult),
            reads=(xb[j], stat_b), writes=(dst_tm_b,))
        for k4 in range(0, KC, 4):
            for t in range(4):
                kc = k4 + t
                ctx.op("pe", lambda e, kc=kc, t=t, dst_tm=dst_tm: e.transpose(
                    pbf(6)[:, t * 128:(t + 1) * 128],
                    dst_tm[:, kc * 128:(kc + 1) * 128], ident),
                    reads=(dst_tm_b, ident_b), writes=(pbank[6],), sem="pe", inc=1)
            vop(lambda e, stg=stg: e.tensor_copy(out=stg, in_=pbf(6)[:, 0:512]),
                reads=(pbank[6],), writes=(stg_b,))
            for t in range(4):
                kc = k4 + t
                vop(lambda e, kc=kc, t=t, col=col, dst_fm=dst_fm, stg=stg:
                    e.tensor_copy(out=dst_fm[kc][:, col:col + 128],
                                  in_=stg[:, t * 128:(t + 1) * 128]),
                    reads=(stg_b,), writes=(dst_fm_b[kc],))

    for j in range(NT):
        dma("gp", x_tm[j], x_in[j * 128:(j + 1) * 128, :], writes=(xb[j],))
    ctx.op("gp", lambda e: e.memset(ident, 0.0), writes=(ident_b,), sem="gp", inc=1)
    ctx.op("gp", lambda e: e.affine_select(out=ident, in_=ident, compare_op=ALU.not_equal,
                                           fill=1.0, base=0, pattern=[[-1, 128]],
                                           channel_multiplier=1),
           reads=(ident_b,), writes=(ident_b,), sem="gp", inc=1)
    ctx.op("gp", lambda e: e.memset(ones_row, 1.0), writes=(ones_b,), sem="gp", inc=1)
    dma("gp", selb[0:1, :], selb_in[:, :], writes=(selb_b,))
    vop(lambda e: e.tensor_copy(out=selb_bf[0:1, :], in_=selb[0:1, :]),
        reads=(selb_b,), writes=(selb_bfb,))

    ar.push()
    bo_f = ar.alloc(D, F32)
    bo_fb = Buf("bo_f")
    dma("gp", bo_f[0:1, :], bo_in[:, :], writes=(bo_fb,))
    vop(lambda e: e.tensor_copy(out=bo_bf[0:1, :], in_=bo_f[0:1, :]),
        reads=(bo_fb,), writes=(bo_bb,))
    sc_f16 = ar.alloc(16 * 11, F16)
    sc_b = Buf("sc_f16")
    dma("gp", sc_f16.rearrange("p (j g) -> p j g", g=16),
        gus_in.rearrange("(j p) g -> p j g", p=128), writes=(sc_b,))
    vop(lambda e: e.tensor_copy(out=gus_sc, in_=sc_f16), reads=(sc_b,), writes=(gus_b,))
    scd_f16 = ar.alloc(16 * 6, F16)
    scd_b = Buf("scd_f16")
    dma("gp", scd_f16.rearrange("p (j g) -> p j g", g=6),
        dns_in.rearrange("(j p) g -> p j g", p=128), writes=(scd_b,))
    vop(lambda e: e.tensor_copy(out=dns_sc, in_=scd_f16), reads=(scd_b,), writes=(dns_b,))

    ar.pop()

    dn_lin_b = Buf("dn_lin")
    attn_lin_b = Buf("attn_lin")
    wshb = {w: Buf(f"{w}shb") for w in ("wqt", "wkt", "wvt", "wot")}

    def conv_jobs(cva):
        wcv_f, wcv_fb, wcv_bf, wcv_bb = cva
        jobs = []
        for wname, win in (("wqt", wq_in), ("wkt", wk_in), ("wvt", wv_in)):
            for s in range(2):
                def cj(wname=wname, win=win, s=s):
                    dma("sc", wcv_f.rearrange("p (k m) -> p k m", m=128),
                        win.rearrange("(k p) m -> p k m", p=128)
                        [:, :, s * 128:(s + 1) * 128], writes=(wcv_fb,))
                    vop(lambda e, wcv_bf=wcv_bf, wcv_f=wcv_f:
                        e.tensor_copy(out=wcv_bf, in_=wcv_f),
                        reads=(wcv_fb,), writes=(wcv_bb,))
                    dma("sc", wsh[wname].rearrange("n (p f) -> n p f", p=128)[s],
                        wcv_bf, reads=(wcv_bb,), writes=(wshb[wname],))
                jobs.append(cj)
        for s in range(2):
            def cjo(s=s):
                dma("sc", wcv_f, wo_in[s * 128:(s + 1) * 128, :], writes=(wcv_fb,))
                vop(lambda e, wcv_bf=wcv_bf, wcv_f=wcv_f:
                    e.tensor_copy(out=wcv_bf, in_=wcv_f),
                    reads=(wcv_fb,), writes=(wcv_bb,))
                dma("sc", wsh["wot"].rearrange("n (p f) -> n p f", p=128)[s], wcv_bf,
                    reads=(wcv_bb,), writes=(wshb["wot"],))
            jobs.append(cjo)
        jobs.append(lambda: coll(lambda e: e.collective_compute(
            "AllGather", ALU.bypass, replica_groups=[list(range(NCORE))],
            ins=[wsh_all.tensor.ap().opt()], outs=[w_lin_all.tensor.ap().opt()]),
            reads=tuple(wshb.values()), writes=(attn_lin_b,)))
        return jobs

    def emit_dequant_dve(packed_ap, row_off, n_i32, scale_cols, scale_b,
                         scale_col_off, dqa, pb):
        w_t = dqa["w"]
        nib = dqa["nib"]
        lo3 = dqa["lo3"]
        sgn = dqa["sgn"]
        bits = dqa["bits"]
        lo3f = dqa["lo3f"]
        corr = dqa["corr"]
        dq_bf = dqa["dqbf"][pb]
        dqbfb = dqa["dqbfb"][pb]
        dma("sy", w_t[:, 0:n_i32], packed_ap[row_off:row_off + 128, 0:n_i32],
            writes=(dqa["wb"],))
        n_sub = (n_i32 + DQ_SUB_I32 - 1) // DQ_SUB_I32
        for s in range(n_sub):
            c0 = s * DQ_SUB_I32
            c1 = min(n_i32, c0 + DQ_SUB_I32)
            nw = (c1 - c0) * 8
            o0 = c0 * 8
            nibv = nib[:, 0:nw].rearrange("p (g e) -> p g e", e=8)
            for jn in range(8):
                vop(lambda e, jn=jn, c0=c0, c1=c1, nibv=nibv:
                    e.tensor_scalar(out=nibv[:, :, jn], in0=w_t[:, c0:c1],
                                    scalar1=4 * jn, scalar2=15,
                                    op0=ALU.logical_shift_right, op1=ALU.bitwise_and),
                    reads=(dqa["wb"],), writes=(dqa["nibb"],))
            vop(lambda e, nw=nw: e.tensor_scalar(out=lo3[:, 0:nw], in0=nib[:, 0:nw],
                scalar1=7, scalar2=None, op0=ALU.bitwise_and),
                reads=(dqa["nibb"],), writes=(dqa["lo3b"],))
            vop(lambda e, nw=nw: e.tensor_scalar(out=sgn[:, 0:nw], in0=nib[:, 0:nw],
                scalar1=8, scalar2=28, op0=ALU.bitwise_and, op1=ALU.logical_shift_left),
                reads=(dqa["nibb"],), writes=(dqa["sgnb"],))
            vop(lambda e, nw=nw: e.tensor_scalar(out=bits[:, 0:nw], in0=lo3[:, 0:nw],
                scalar1=0xFC, scalar2=(1 << 22), op0=ALU.add, op1=ALU.mult),
                reads=(dqa["lo3b"],), writes=(dqa["bitsb"],))
            vop(lambda e, nw=nw: e.tensor_scalar(out=lo3f[:, 0:nw],
                in0=bits[:, 0:nw].bitcast(F32), scalar1=-1.0, scalar2=0.0,
                op0=ALU.add, op1=ALU.min),
                reads=(dqa["bitsb"],), writes=(dqa["lo3fb"],))
            vop(lambda e, nw=nw: e.tensor_tensor(out=corr[:, 0:nw],
                in0=bits[:, 0:nw].bitcast(F32), in1=lo3f[:, 0:nw], op=ALU.add),
                reads=(dqa["bitsb"], dqa["lo3fb"]), writes=(dqa["corrb"],),
                stream="gp")
            vop(lambda e, nw=nw: e.tensor_tensor(out=lo3f[:, 0:nw].bitcast(I32),
                in0=corr[:, 0:nw].bitcast(I32), in1=sgn[:, 0:nw], op=ALU.bitwise_or),
                reads=(dqa["corrb"], dqa["sgnb"]), writes=(dqa["lo3fb"],))
            g0 = o0 // 128
            for g in range(nw // 128):
                vop(lambda e, g=g, o0=o0, g0=g0, scale_cols=scale_cols:
                    e.tensor_scalar(out=dq_bf[:, o0 + g * 128:o0 + (g + 1) * 128],
                                    in0=lo3f[:, g * 128:(g + 1) * 128],
                                    scalar1=scale_cols[:, scale_col_off + g0 + g:
                                                       scale_col_off + g0 + g + 1],
                                    scalar2=None, op0=ALU.mult),
                    reads=(dqa["lo3fb"], scale_b), writes=(dqbfb,))

    def emit_dequant_tail(n_i32, tile_outs, dqa, pb):
        dq_bf = dqa["dqbf"][pb]
        dqbfb = dqa["dqbfb"][pb]
        stg = dqa["stg"]
        n_kc = n_i32 * 8 // 128
        for kc4 in range(0, n_kc, 4):
            kn = min(4, n_kc - kc4)
            for t in range(kn):
                kc = kc4 + t
                ctx.op("pe", lambda e, kc=kc, t=t, dq_bf=dq_bf: e.transpose(
                    pbf(7)[:, t * 128:(t + 1) * 128],
                    dq_bf[:, kc * 128:(kc + 1) * 128], ident),
                    reads=(dqbfb, ident_b), writes=(pbank[7],), sem="pe", inc=1)
            vop(lambda e, kn=kn, stg=stg: e.copy(
                out=stg[:, 0:kn * 128], in_=pbf(7)[:, 0:kn * 128]),
                reads=(pbank[7],), writes=(dqa["stgb"],), stream="sc")
            contig = kn > 1 and all(tile_outs[kc4 + t] is None for t in range(1, kn))
            if contig:
                dma("sy", tile_outs[kc4], stg[:, 0:kn * 128],
                    reads=(dqa["stgb"],), writes=(dqa["shb"],))
            else:
                for t in range(kn):
                    dma("sy", tile_outs[kc4 + t], stg[:, t * 128:(t + 1) * 128],
                        reads=(dqa["stgb"],), writes=(dqa["shb"],))

    def dq_alloc():
        dqa = {"shb": Buf("dqsh")}
        dqa["w"] = ar.alloc(704, I32)
        dqa["wb"] = Buf("dqw")
        nw = DQ_SUB_I32 * 8
        for nm in ("nib", "lo3", "sgn", "bits"):
            dqa[nm] = ar.alloc(nw, I32)
            dqa[nm + "b"] = Buf(nm)
        dqa["lo3f"] = ar.alloc(nw, F32)
        dqa["lo3fb"] = Buf("lo3f")
        dqa["corr"] = ar.alloc(nw, F32)
        dqa["corrb"] = Buf("corr")
        dqa["dqbf"] = [ar.alloc(IFF, BF16), ar.alloc(IFF, BF16)]
        dqa["dqbfb"] = [Buf("dqbf0"), Buf("dqbf1")]
        dqa["stg"] = ar.alloc(512, BF16)
        dqa["stgb"] = Buf("dqstg")
        return dqa

    gu_gather_b = [Buf(f"gugather{i}") for i in range(len(GU_SPLITS))]

    def dq_jobs(dqa):
        chunks = []
        for i, (a, b) in enumerate(GU_SPLITS):
            for j in range(a, b):
                def gup_dve(pb, i=i, j=j):
                    emit_dequant_dve(gup_in, j * 128, 256, gus_sc, gus_b, j * 16,
                                     dqa, pb)

                def gup_tail(pb, i=i, j=j, a=a):
                    panel = gu_sh[i].rearrange("n (p f) -> n p f", p=128)[j - a]
                    touts = []
                    for kc4 in range(0, 16, 4):
                        touts.append(panel[:, kc4 * 128:(kc4 + 4) * 128])
                        touts += [None, None, None]
                    emit_dequant_tail(256, touts, dqa, pb)
                chunks.append((gup_dve, gup_tail, None))
            chunks[-1] = (chunks[-1][0], chunks[-1][1],
                          lambda i=i: coll(lambda e, i=i: e.collective_compute(
                              "AllGather", ALU.bypass,
                              replica_groups=[list(range(NCORE))],
                              ins=[gu_sh[i].tensor.ap().opt()],
                              outs=[gu_all[i].tensor.ap().opt()]),
                              reads=(dqa["shb"],), writes=(gu_gather_b[i],)))
        dn_panels = dn_sh.rearrange("n (p f) -> n p f", p=128)
        dn_chunks = []
        for fk in range(16):
            def dn_dve(pb, fk=fk):
                emit_dequant_dve(dnp_in, fk * 128, 96, dns_sc, dns_b, fk * 6,
                                 dqa, pb)

            def dn_tail(pb, fk=fk):
                touts = [dn_panels[s][:, fk * 128:(fk + 1) * 128] for s in range(6)]
                emit_dequant_tail(96, touts, dqa, pb)
            fin = None
            if fk == 15:
                def fin():
                    coll(lambda e: e.collective_compute(
                        "AllGather", ALU.bypass, replica_groups=[list(range(NCORE))],
                        ins=[dn_sh.tensor.ap().opt()], outs=[dn_all.tensor.ap().opt()]),
                        reads=(dqa["shb"],), writes=(dn_lin_b,))
            dn_chunks.append((dn_dve, dn_tail, fin))
        head, rest = chunks[:4], chunks[4:]
        merged = list(head)
        di = 0
        for ch in rest:
            for _ in range(3):
                if di < len(dn_chunks):
                    merged.append(dn_chunks[di])
                    di += 1
            merged.append(ch)
        merged.extend(dn_chunks[di:])
        chunks = merged
        jobs = []
        for k in range(len(chunks) + 1):
            def job(k=k):
                if k < len(chunks):
                    chunks[k][0](k % 2)
                if k > 0:
                    chunks[k - 1][1]((k - 1) % 2)
                    if chunks[k - 1][2] is not None:
                        chunks[k - 1][2]()
            jobs.append(job)
        return jobs

    def emit_ffn(segs, dodump=False, with_jobs=False, nwg=3):
        ntt = 4
        TW = ntt * 128
        nb = TW // 512
        ar.push()
        dqa = dq_alloc() if with_jobs else None
        cva = None
        if with_jobs:
            cva = (dqa["dqbf"][0].bitcast(F32)[:, 0:D], dqa["dqbfb"][0],
                   dqa["dqbf"][1][:, 0:D], dqa["dqbfb"][1])
        xn_fm = [ar.alloc(TW, BF16) for _ in range(KC)]
        xnb = [Buf(f"xn{k}") for k in range(KC)]
        h_fm = [ar.alloc(TW, BF16) for _ in range(ICD)]
        hb = [Buf(f"h{i}") for i in range(ICD)]
        xn_tm = ar.alloc(D, BF16)
        xn_tm_b = Buf("xn_tm")
        NWG = nwg
        wg_sb = [ar.alloc(D, BF16) for _ in range(NWG)]
        wg_b = [Buf(f"wg{i}") for i in range(NWG)]
        sg_sb = [ar.alloc(TW, F32) for _ in range(2)]
        sg_b = [Buf(f"sg{i}") for i in range(2)]
        stg = ar.alloc(512, BF16)
        stg_b = Buf("tstg")

        for si, tok0 in enumerate(segs):
            seg_jobs = []
            if with_jobs:
                seg_jobs = dq_jobs(dqa) if si == 0 else conv_jobs(cva)
            emit_ffn_seg(si, tok0, seg_jobs, dodump and si == 0 and tok0 == 0,
                         TW, nb, xn_fm, xnb, h_fm, hb, xn_tm, xn_tm_b,
                         NWG, wg_sb, wg_b, sg_sb, sg_b, stg, stg_b)
        ar.pop()

    def emit_ffn_seg(si, tok0, jobs, dodump, TW, nb, xn_fm, xnb, h_fm, hb,
                     xn_tm, xn_tm_b, NWG, wg_sb, wg_b, sg_sb, sg_b, stg, stg_b):
        ntt = TW // 128
        if dodump:
            dump("x0", x_tm[0], (xb[0],))
        for jt in range(ntt):
            emit_ln(tok0 // 128 + jt, xn_tm, xn_tm_b, xn_fm, xnb, jt * 128, stg, stg_b)
            if dodump and jt == 0:
                dump("xn0", xn_tm, (xn_tm_b,))
                dump("st0", statc, (stat_b,))
        if dodump:
            dump("xnf0", xn_fm[0], (xnb[0],))

        ji = 0
        if si == 0 and jobs:
            na, nb_ = GU_SPLITS[0]
            while ji <= nb_ - na:
                jobs[ji]()
                ji += 1
        pair_seq = [(i, c, j) for i, (a, b) in enumerate(GU_SPLITS)
                    for c in range(4) for j in range(a, b)]
        for pi, (i, c, j) in enumerate(pair_seq):
            for _ in range(2):
                if ji < len(jobs):
                    jobs[ji]()
                    ji += 1
            a, b = GU_SPLITS[i]
            oc = c * 11 + j
            for half in range(2):
                cc = c + 4 * half
                buf = (2 * pi + half) % NWG
                r0 = cc * (b - a) + (j - a)
                dma("sy" if half == 0 else "gp", wg_sb[buf],
                    gu_all[i].rearrange("n (p f) -> n p f", p=128)[r0],
                    reads=(gu_gather_b[i],), writes=(wg_b[buf],))
                bank0 = (0 if half == 0 else nb) + (0 if pi % 2 == 0 else 2 * nb)
                pw = [pbank[bank0 + q] for q in range(nb)]
                for kc in range(KC):
                    for q in range(nb):
                        ctx.op("pe", lambda e, kc=kc, q=q, buf=buf, bank0=bank0,
                               wg_sb=wg_sb, xn_fm=xn_fm:
                               e.matmul(psum[:, (bank0 + q) * 512:(bank0 + q + 1) * 512],
                                        wg_sb[buf][:, kc * 128:(kc + 1) * 128],
                                        xn_fm[kc][:, q * 512:(q + 1) * 512],
                                        start=(kc == 0), stop=(kc == KC - 1)),
                               reads=(wg_b[buf], xnb[kc]),
                               writes=tuple(pw) if kc in (0, KC - 1) else (),
                               sem="pe", inc=1 if kc == KC - 1 else 0)
                if half == 0:
                    sbuf = pi % 2
                    ctx.op("sc", lambda e, sbuf=sbuf, bank0=bank0, nb=nb, sg_sb=sg_sb:
                           e.activation(out=sg_sb[sbuf],
                                        in_=psum[:, bank0 * 512:(bank0 + nb) * 512],
                                        func=AF.Silu),
                           reads=tuple(pw), writes=(sg_b[sbuf],), sem="sc", inc=1)
                else:
                    sbuf = pi % 2
                    vop(lambda e, sbuf=sbuf, oc=oc, bank0=bank0, nb=nb, sg_sb=sg_sb,
                        h_fm=h_fm: e.tensor_tensor(
                            out=h_fm[oc], in0=sg_sb[sbuf],
                            in1=psum[:, bank0 * 512:(bank0 + nb) * 512], op=ALU.mult),
                        reads=(sg_b[sbuf],) + tuple(pw), writes=(hb[oc],))

        while ji < len(jobs):
            jobs[ji]()
            ji += 1
        if dodump:
            dump("h0", h_fm[0], (hb[0],))
            dump("h43", h_fm[43], (hb[43],))
        for tp in range(ntt // 2):
            pdn = ([pbank[q] for q in range(4)], [pbank[4 + q] for q in range(4)])
            for ic in range(ICD):
                buf = ic % NWG
                if ic < 24:
                    row = (ic // 6) * 6 + ic % 6
                else:
                    row = 24 + ((ic - 24) // 5) * 6 + (ic - 24) % 5
                dma("sy" if ic % 2 == 0 else "gp", wg_sb[buf],
                    dn_all.rearrange("n (p f) -> n p f", p=128)[row],
                    reads=(dn_lin_b,), writes=(wg_b[buf],))
                for half in range(2):
                    jt = tp * 2 + half
                    for q in range(4):
                        last = (half == 1 and q == 3)
                        ctx.op("pe", lambda e, ic=ic, buf=buf, half=half, jt=jt, q=q,
                               wg_sb=wg_sb, h_fm=h_fm:
                               e.matmul(psum[:, (half * 4 + q) * 512:
                                              (half * 4 + q + 1) * 512],
                                        h_fm[ic][:, jt * 128:(jt + 1) * 128],
                                        wg_sb[buf][:, q * 512:(q + 1) * 512],
                                        start=(ic == 0), stop=(ic == ICD - 1)),
                               reads=(wg_b[buf], hb[ic]),
                               writes=(tuple(pdn[0]) + tuple(pdn[1]))
                               if ((ic == 0 and half == 0 and q == 0) or
                                   (ic == ICD - 1 and last)) else (),
                               sem="pe", inc=1 if last else 0)
            for half in range(2):
                jt = tp * 2 + half
                j = tok0 // 128 + jt
                pw = pdn[half]
                sqa = statc[:, 4:5]
                sqb = statc[:, 5:6]
                rr = statc[:, 6:7]
                ctx.op("sc", lambda e, half=half, sqa=sqa: e.activation(
                    out=trash, in_=psum[:, half * 2048:half * 2048 + 1024],
                    func=AF.Square, accum_out=sqa),
                    reads=tuple(pw) + (trash_b,), writes=(trash_b, stat_b),
                    sem="sc", inc=1)
                ctx.op("sc", lambda e, half=half, sqb=sqb: e.activation(
                    out=trash, in_=psum[:, half * 2048 + 1024:half * 2048 + 2048],
                    func=AF.Square, accum_out=sqb),
                    reads=tuple(pw) + (trash_b,), writes=(trash_b, stat_b),
                    sem="sc", inc=1)
                vop(lambda e, sqa=sqa, sqb=sqb: e.tensor_tensor(out=sqa, in0=sqa,
                    in1=sqb, op=ALU.add), reads=(stat_b,), writes=(stat_b,))
                vop(lambda e, sqa=sqa: e.tensor_scalar(out=sqa, in0=sqa,
                    scalar1=1.0 / D, scalar2=EPS_RMS, op0=ALU.mult, op1=ALU.add),
                    reads=(stat_b,), writes=(stat_b,))
                ctx.op("sc", lambda e, sqa=sqa, rr=rr: e.activation(
                    out=rr, in_=sqa, func=AF.Sqrt),
                    reads=(stat_b,), writes=(stat_b,), sem="sc", inc=1)
                vop(lambda e, rr=rr: e.reciprocal(out=rr, in_=rr),
                    reads=(stat_b,), writes=(stat_b,))
                vop(lambda e, rr=rr: e.tensor_scalar(out=rr, in0=rr, scalar1=0.5,
                    scalar2=None, op0=ALU.mult), reads=(stat_b,), writes=(stat_b,))
                vop(lambda e, half=half, j=j, rr=rr: e.scalar_tensor_tensor(
                    out=x_tm[j], in0=psum[:, half * 2048:(half + 1) * 2048],
                    scalar=rr, in1=x_tm[j], op0=ALU.mult, op1=ALU.add),
                    reads=(stat_b, xb[j]) + tuple(pw), writes=(xb[j],))

    def emit_attn():
        ar.push()
        qkvn_fm = [ar.alloc(T, BF16) for _ in range(KC)]
        qkvn_fmb = [Buf(f"qkvnf{k}") for k in range(KC)]
        kvbd = [ar.alloc(128, BF16) for _ in range(NH2)]
        kvbd_b = [Buf(f"kvbd{c}") for c in range(NH2)]
        kscols = ar.alloc(32, BF16)
        kscols_b = Buf("kscols")
        qkvn = ar.alloc(D, BF16)
        qkvn_b = Buf("qkvn")
        rn = ar.alloc(32, F32)
        rn_b = Buf("rn")
        selc = ar.alloc(2, F32)
        selc_b = Buf("selc")

        ar.push()
        xn2_fm = [ar.alloc(T, BF16) for _ in range(KC)]
        xn2b = [Buf(f"xn2{k}") for k in range(KC)]
        ar.push()
        xn_tm2 = ar.alloc(D, BF16)
        xn_tm2_b = Buf("xn_tm2")
        stg2 = ar.alloc(512, BF16)
        stg2_b = Buf("stg2")
        for j in range(NT):
            emit_ln(j, xn_tm2, xn_tm2_b, xn2_fm, xn2b, j * 128, stg2, stg2_b)
        ar.pop()
        ctx.barrier()

        q_fm = [ar.alloc(T, BF16) for _ in range(KC)]
        qfb = [Buf(f"qf{m}") for m in range(KC)]
        kv_ws = [ar.alloc(D, BF16) for _ in range(2)]
        kv_wsb = [Buf(f"kvws{i}") for i in range(2)]

        def wtile_dma(wname, mc, buf):
            dma("sy", kv_ws[buf],
                w_lin_all.rearrange("n (p f) -> n p f", p=128)[w_lin_row(wname, mc)],
                reads=(attn_lin_b,), writes=(kv_wsb[buf],))

        ctx.op("pe", lambda e: e.matmul(psum[:, 7 * 512:7 * 512 + 2],
                                        ones_row[0:1, :], selb_bf[0:1, :],
                                        start=True, stop=True),
               reads=(ones_b, selb_bfb), writes=(pbank[7],), sem="pe", inc=1)
        vop(lambda e: e.tensor_copy(out=selc, in_=psum[:, 7 * 512:7 * 512 + 2]),
            reads=(pbank[7],), writes=(selc_b,))

        ar.push()
        kc_sb = [ar.alloc(T, BF16) for _ in range(2)]
        kc_b = [Buf(f"kc{i}") for i in range(2)]
        vc_sb = [ar.alloc(T, BF16) for _ in range(2)]
        vc_b = [Buf(f"vc{i}") for i in range(2)]
        ks_sb = ar.alloc(16, F32)
        ks_b = Buf("ks")
        kvps = [pbank[4 + q] for q in range(4)]
        NTMP = 4
        ktmp = [ar.alloc(128, BF16) for _ in range(NTMP)]
        ktmp_b = [Buf(f"ktmp{i}") for i in range(NTMP)]
        vtmp = [ar.alloc(128, BF16) for _ in range(NTMP)]
        vtmp_b = [Buf(f"vtmp{i}") for i in range(NTMP)]

        def kv_prod(c16):
            pr = c16 % 2
            kb0 = 0 if pr == 0 else 2
            vb0 = 2 if pr == 0 else 0
            for wi, (wname, dst, dstb, actf, bank0) in enumerate(
                    (("wkt", kc_sb[pr], kc_b[pr], AF.Relu, kb0),
                     ("wvt", vc_sb[pr], vc_b[pr], AF.Identity, vb0))):
                buf = (2 * c16 + wi) % 2
                wtile_dma(wname, c16, buf)
                pw = [pbank[bank0], pbank[bank0 + 1]]
                for kc in range(KC):
                    for q in range(2):
                        ctx.op("pe", lambda e, kc=kc, q=q, buf=buf, bank0=bank0,
                               xn2_fm=xn2_fm:
                               e.matmul(psum[:, (bank0 + q) * 512:(bank0 + q + 1) * 512],
                                        kv_ws[buf][:, kc * 128:(kc + 1) * 128],
                                        xn2_fm[kc][:, q * 512:(q + 1) * 512],
                                        start=(kc == 0), stop=(kc == KC - 1)),
                               reads=(kv_wsb[buf], xn2b[kc]),
                               writes=tuple(pw) if kc in (0, KC - 1) else (),
                               sem="pe", inc=1 if kc == KC - 1 else 0)
                ctx.op("sc", lambda e, dst=dst, actf=actf, bank0=bank0: e.activation(
                    out=dst, in_=psum[:, bank0 * 512:(bank0 + 2) * 512], func=actf),
                    reads=tuple(pw), writes=(dstb,), sem="sc", inc=1)
            vop(lambda e, pr=pr, c16=c16: e.tensor_reduce(
                out=ks_sb[:, c16:c16 + 1], in_=kc_sb[pr], axis=AX.X, op=ALU.add),
                reads=(kc_b[pr],), writes=(ks_b,))

        def kv_tp(c16, jt):
            pr = c16 % 2
            kb0 = 0 if pr == 0 else 2
            tb = (c16 * NT + jt) % NTMP
            off = (tb % 2) * 256
            ctx.op("pe", lambda e, pr=pr, jt=jt, kb0=kb0, off=off: e.transpose(
                pbf(kb0)[:, off:off + 128],
                kc_sb[pr][:, jt * 128:(jt + 1) * 128], ident),
                reads=(kc_b[pr], ident_b), writes=(pbank[kb0],), sem="pe", inc=1)
            ctx.op("pe", lambda e, pr=pr, jt=jt, kb0=kb0, off=off: e.transpose(
                pbf(kb0)[:, off + 128:off + 256],
                vc_sb[pr][:, jt * 128:(jt + 1) * 128], ident),
                reads=(vc_b[pr], ident_b), writes=(pbank[kb0],), sem="pe", inc=1)
            vop(lambda e, tb=tb, kb0=kb0, off=off: e.tensor_copy(
                out=ktmp[tb], in_=pbf(kb0)[:, off:off + 128]),
                reads=(pbank[kb0],), writes=(ktmp_b[tb],))
            vop(lambda e, tb=tb, kb0=kb0, off=off: e.tensor_copy(
                out=vtmp[tb], in_=pbf(kb0)[:, off + 128:off + 256]),
                reads=(pbank[kb0],), writes=(vtmp_b[tb],))

        def kv_mm(c16, jt):
            tb = (c16 * NT + jt) % NTMP
            ctx.op("pe", lambda e, c16=c16, jt=jt, tb=tb: e.matmul(
                psum[:, 2048 + c16 * 128:2048 + (c16 + 1) * 128],
                ktmp[tb], vtmp[tb], start=(jt == 0), stop=(jt == NT - 1)),
                reads=(ktmp_b[tb], vtmp_b[tb]),
                writes=tuple(kvps) if (c16 == 0 and jt == 0) or
                (c16 == NH2 - 1 and jt == NT - 1) else (),
                sem="pe", inc=1)

        steps = []
        for c16 in range(NH2):
            steps.append(("prod", c16, None))
            for jt in range(NT):
                steps.append(("tp", c16, jt))
        mmq = []
        for kind, c16, jt in steps:
            if kind == "prod":
                kv_prod(c16)
            else:
                kv_tp(c16, jt)
                mmq.append((c16, jt))
                if len(mmq) > 2:
                    kv_mm(*mmq.pop(0))
        while mmq:
            kv_mm(*mmq.pop(0))

        kvpart = ar.alloc(NH2 * 64, F32)
        kvp_b = Buf("kvpart")
        kvsc = ar.alloc(NH2 * 64, F32)
        kvsc_b = Buf("kvsc")
        kssc = ar.alloc(16, F32)
        kssc_b = Buf("kssc")
        for c16 in range(NH2):
            vop(lambda e, c16=c16: e.tensor_copy(
                out=kvpart[0:64, c16 * 64:(c16 + 1) * 64],
                in_=psum[0:64, 2048 + c16 * 128:2048 + c16 * 128 + 64]),
                reads=(kvps[c16 // 4],), writes=(kvp_b,))
            vop(lambda e, c16=c16: e.tensor_copy(
                out=kvpart[64:128, c16 * 64:(c16 + 1) * 64],
                in_=psum[64:128, 2048 + c16 * 128 + 64:2048 + c16 * 128 + 128]),
                reads=(kvps[c16 // 4],), writes=(kvp_b,))
        kvcci_b = Buf("kvcci")
        kscci_b = Buf("kscci")
        for b in range(2):
            vop(lambda e, b=b: e.tensor_scalar(out=kvsc, in0=kvpart,
                scalar1=selc[:, b:b + 1], scalar2=None, op0=ALU.mult),
                reads=(kvp_b, selc_b), writes=(kvsc_b,))
            vop(lambda e, b=b: e.tensor_scalar(out=kssc, in0=ks_sb,
                scalar1=selc[:, b:b + 1], scalar2=None, op0=ALU.mult),
                reads=(ks_b, selc_b), writes=(kssc_b,))
            dma("gp", kv_cci[b * 2048:(b + 1) * 2048, :].rearrange(
                "(c p) e -> p c e", p=128),
                kvsc.rearrange("p (c e) -> p c e", e=64),
                reads=(kvsc_b,), writes=(kvcci_b,))
            dma("gp", ks_cci[b * 128:(b + 1) * 128, :], kssc,
                reads=(kssc_b,), writes=(kscci_b,))
        cc_b = Buf("kvar")
        coll(lambda e: e.collective_compute(
            "AllReduce", ALU.add, replica_groups=[list(range(NCORE))],
            ins=[kv_cci.tensor.ap().opt()], outs=[kv_cco.tensor.ap().opt()]),
            reads=(kvcci_b,), writes=(cc_b,))
        coll(lambda e: e.collective_compute(
            "AllReduce", ALU.add, replica_groups=[list(range(NCORE))],
            ins=[ks_cci.tensor.ap().opt()], outs=[ks_cco.tensor.ap().opt()]),
            reads=(kscci_b,), writes=(cc_b,))

        for mc in range(KC):
            buf = mc % 2
            wtile_dma("wqt", mc, buf)
            bank0 = 0 if mc % 2 == 0 else 2
            pw = [pbank[bank0], pbank[bank0 + 1]]
            for kc in range(KC):
                for q in range(2):
                    ctx.op("pe", lambda e, kc=kc, q=q, buf=buf, bank0=bank0,
                           xn2_fm=xn2_fm:
                           e.matmul(psum[:, (bank0 + q) * 512:(bank0 + q + 1) * 512],
                                    kv_ws[buf][:, kc * 128:(kc + 1) * 128],
                                    xn2_fm[kc][:, q * 512:(q + 1) * 512],
                                    start=(kc == 0), stop=(kc == KC - 1)),
                           reads=(kv_wsb[buf], xn2b[kc]),
                           writes=tuple(pw) if kc in (0, KC - 1) else (),
                           sem="pe", inc=1 if kc == KC - 1 else 0)
            ctx.op("sc", lambda e, mc=mc, bank0=bank0, q_fm=q_fm: e.activation(
                out=q_fm[mc], in_=psum[:, bank0 * 512:(bank0 + 2) * 512], func=AF.Relu),
                reads=tuple(pw), writes=(qfb[mc],), sem="sc", inc=1)

        kvld = [kvpart, kvsc]
        kvld_b = [kvp_b, kvsc_b]
        ksld = [ks_sb, kssc]
        ksld_b = [ks_b, kssc_b]
        for b in range(2):
            dma("gp", kvld[b].rearrange("p (c e) -> p c e", e=64),
                kv_cco[b * 2048:(b + 1) * 2048, :].rearrange("(c p) e -> p c e", p=128),
                reads=(cc_b,), writes=(kvld_b[b],))
            dma("gp", ksld[b], ks_cco[b * 128:(b + 1) * 128, :],
                reads=(cc_b,), writes=(ksld_b[b],))
        vop(lambda e: e.tensor_scalar(out=kvld[0], in0=kvld[0], scalar1=selc[:, 0:1],
            scalar2=None, op0=ALU.mult), reads=(kvld_b[0], selc_b), writes=(kvld_b[0],))
        vop(lambda e: e.scalar_tensor_tensor(out=kvld[0], in0=kvld[1],
            scalar=selc[:, 1:2], in1=kvld[0], op0=ALU.mult, op1=ALU.add),
            reads=(kvld_b[1], kvld_b[0], selc_b), writes=(kvld_b[0],))
        vop(lambda e: e.tensor_scalar(out=ksld[0], in0=ksld[0], scalar1=selc[:, 0:1],
            scalar2=None, op0=ALU.mult), reads=(ksld_b[0], selc_b), writes=(ksld_b[0],))
        vop(lambda e: e.scalar_tensor_tensor(out=ksld[0], in0=ksld[1],
            scalar=selc[:, 1:2], in1=ksld[0], op0=ALU.mult, op1=ALU.add),
            reads=(ksld_b[1], ksld_b[0], selc_b), writes=(ksld_b[0],))
        ctx.op("gp", lambda e: e.memset(kscols, 0.0), writes=(kscols_b,),
               sem="gp", inc=1)
        for c16 in range(NH2):
            ctx.op("gp", lambda e, c16=c16: e.memset(kvbd[c16], 0.0),
                   writes=(kvbd_b[c16],), sem="gp", inc=1)
            vop(lambda e, c16=c16: e.tensor_copy(
                out=kvbd[c16][0:64, 0:64], in_=kvld[0][0:64, c16 * 64:(c16 + 1) * 64]),
                reads=(kvld_b[0],), writes=(kvbd_b[c16],))
            vop(lambda e, c16=c16: e.tensor_copy(
                out=kvbd[c16][64:128, 64:128],
                in_=kvld[0][64:128, c16 * 64:(c16 + 1) * 64]),
                reads=(kvld_b[0],), writes=(kvbd_b[c16],))
            vop(lambda e, c16=c16: e.tensor_copy(
                out=kscols[0:64, 2 * c16:2 * c16 + 1], in_=ksld[0][0:64, c16:c16 + 1]),
                reads=(ksld_b[0],), writes=(kscols_b,))
            vop(lambda e, c16=c16: e.tensor_copy(
                out=kscols[64:128, 2 * c16 + 1:2 * c16 + 2],
                in_=ksld[0][64:128, c16:c16 + 1]),
                reads=(ksld_b[0],), writes=(kscols_b,))
        ar.pop()
        ctx.barrier()

        rn_all = ar.alloc(256, F32)
        rn_all_b = Buf("rn_all")
        for jt in range(NT):
            for c16 in range(NH2):
                first = (jt == 0 and c16 == 0)
                last = (jt == NT - 1 and c16 == NH2 - 1)
                ctx.op("pe", lambda e, c16=c16, jt=jt, q_fm=q_fm: e.matmul(
                    psum[:, 7 * 512 + jt * 32 + 2 * c16:
                         7 * 512 + jt * 32 + 2 * c16 + 2],
                    q_fm[c16][:, jt * 128:(jt + 1) * 128],
                    kscols[:, 2 * c16:2 * c16 + 2], start=True, stop=True),
                    reads=(qfb[c16], kscols_b),
                    writes=(pbank[7],) if (first or last) else (),
                    sem="pe", inc=1 if last else 0)
        vop(lambda e: e.tensor_scalar(out=rn_all, in0=psum[:, 7 * 512:7 * 512 + 256],
            scalar1=CLIP, scalar2=None, op0=ALU.max),
            reads=(pbank[7],), writes=(rn_all_b,))
        vop(lambda e: e.reciprocal(out=rn_all, in_=rn_all),
            reads=(rn_all_b,), writes=(rn_all_b,))

        qkvn2 = ar.alloc(D, BF16)
        qkvn2_b = Buf("qkvn2")
        qkvns = [(qkvn, qkvn_b), (qkvn2, qkvn2_b)]

        def qkv_mm(jt):
            g0 = 0 if jt % 2 == 0 else 4
            pq = [pbank[g0 + q] for q in range(4)]
            for c16 in range(NH2):
                ctx.op("pe", lambda e, c16=c16, jt=jt, g0=g0, q_fm=q_fm: e.matmul(
                    psum[:, g0 * 512 + c16 * 128:g0 * 512 + (c16 + 1) * 128],
                    q_fm[c16][:, jt * 128:(jt + 1) * 128], kvbd[c16],
                    start=True, stop=True),
                    reads=(qfb[c16], kvbd_b[c16]),
                    writes=tuple(pq) if c16 in (0, NH2 - 1) else (),
                    sem="pe", inc=1 if c16 == NH2 - 1 else 0)
            qn, qnb = qkvns[jt % 2]
            for h in range(32):
                vop(lambda e, h=h, g0=g0, jt=jt, qn=qn: e.tensor_scalar(
                    out=qn[:, h * 64:(h + 1) * 64],
                    in0=psum[:, g0 * 512 + h * 64:g0 * 512 + (h + 1) * 64],
                    scalar1=rn_all[:, jt * 32 + h:jt * 32 + h + 1],
                    scalar2=None, op0=ALU.mult),
                    reads=(rn_all_b,) + tuple(pq), writes=(qnb,))

        def qkv_tr(jt):
            g0 = 0 if jt % 2 == 0 else 4
            qn, qnb = qkvns[jt % 2]
            for k4 in range(0, KC, 4):
                for t in range(4):
                    ec = k4 + t
                    ctx.op("pe", lambda e, ec=ec, t=t, g0=g0, qn=qn: e.transpose(
                        pbf(g0)[:, t * 128:(t + 1) * 128],
                        qn[:, ec * 128:(ec + 1) * 128], ident),
                        reads=(qnb, ident_b), writes=(pbank[g0],), sem="pe", inc=1)
                for t in range(4):
                    ec = k4 + t
                    vop(lambda e, ec=ec, t=t, jt=jt, g0=g0, qkvn_fm=qkvn_fm:
                        e.tensor_copy(
                            out=qkvn_fm[ec][:, jt * 128:(jt + 1) * 128],
                            in_=pbf(g0)[:, t * 128:(t + 1) * 128]),
                        reads=(pbank[g0],), writes=(qkvn_fmb[ec],))

        qkv_mm(0)
        for jt in range(1, NT):
            qkv_mm(jt)
            qkv_tr(jt - 1)
        qkv_tr(NT - 1)
        ar.pop()
        ctx.barrier()

        ar.push()
        wo_sb = [ar.alloc(D, BF16) for _ in range(KC)]
        wo_b = [Buf(f"wo{e}") for e in range(KC)]
        for ec in range(KC):
            dma("sy", wo_sb[ec],
                w_lin_all.rearrange("n (p f) -> n p f", p=128)[w_lin_row("wot", ec)],
                reads=(attn_lin_b,), writes=(wo_b[ec],))
        for jt in range(NT):
            g0 = 0 if jt % 2 == 0 else 4
            po = [pbank[g0 + q] for q in range(4)]
            for ec in range(KC):
                for q in range(4):
                    ctx.op("pe", lambda e, ec=ec, q=q, jt=jt, g0=g0, qkvn_fm=qkvn_fm,
                           wo_sb=wo_sb: e.matmul(
                        psum[:, (g0 + q) * 512:(g0 + q + 1) * 512],
                        qkvn_fm[ec][:, jt * 128:(jt + 1) * 128],
                        wo_sb[ec][:, q * 512:(q + 1) * 512],
                        start=(ec == 0), stop=False),
                        reads=(qkvn_fmb[ec], wo_b[ec]),
                        writes=tuple(po) if ec == 0 else (),
                        sem="pe", inc=0)
            for q in range(4):
                ctx.op("pe", lambda e, q=q, g0=g0: e.matmul(
                    psum[:, (g0 + q) * 512:(g0 + q + 1) * 512], ones_row[0:1, :],
                    bo_bf[0:1, q * 512:(q + 1) * 512], start=False, stop=True),
                    reads=(ones_b, bo_bb),
                    writes=tuple(po) if q == 3 else (),
                    sem="pe", inc=1 if q == 3 else 0)
            vop(lambda e, jt=jt, g0=g0: e.tensor_tensor(
                out=x_tm[jt], in0=psum[:, g0 * 512:g0 * 512 + 2048],
                in1=x_tm[jt], op=ALU.add), reads=(xb[jt],) + tuple(po),
                writes=(xb[jt],))
        ar.pop()
        ar.pop()

    ctx.barrier()
    if "ffn1" in phases:
        emit_ffn([0, 512], dodump=True, with_jobs=True)
        ctx.barrier()
    if "attn" in phases:
        emit_attn()
        ctx.barrier()
    if "ffn3" in phases:
        emit_ffn([0, 512], nwg=9)
        ctx.barrier()
    for j in range(NT):
        dma("gp", y_out[j * 128:(j + 1) * 128, :], x_tm[j], reads=(xb[j],),
            writes=(Buf(f"y{j}"),))
    for i in range(8):
        ctx._wait("gp", f"qgp{i}", ctx.counts[f"qgp{i}"])
    ctx.emit()
    return nc


def make_in_maps(L0_attn_hidden_states, L0_attn_Wq, L0_attn_Wk, L0_attn_Wv,
                 L0_attn_Wo, L0_attn_bo, L0_ffn_gate_up_packed,
                 L0_ffn_gate_up_scales, L0_ffn_down_packed, L0_ffn_down_scales):
    x = np.ascontiguousarray(np.asarray(L0_attn_hidden_states, dtype=np.float32)
                             .reshape(NCORE * T, D))
    wq = np.ascontiguousarray(np.asarray(L0_attn_Wq, dtype=np.float32))
    wk = np.ascontiguousarray(np.asarray(L0_attn_Wk, dtype=np.float32))
    wv = np.ascontiguousarray(np.asarray(L0_attn_Wv, dtype=np.float32))
    wo = np.ascontiguousarray(np.asarray(L0_attn_Wo, dtype=np.float32))
    bo = np.ascontiguousarray(np.asarray(L0_attn_bo, dtype=np.float32)).reshape(1, D)
    gup = np.ascontiguousarray(np.asarray(L0_ffn_gate_up_packed, dtype=np.int32))
    gus = np.ascontiguousarray(np.asarray(L0_ffn_gate_up_scales, dtype=np.float16).T)
    dnp = np.ascontiguousarray(np.asarray(L0_ffn_down_packed, dtype=np.int32))
    dns = np.ascontiguousarray(np.asarray(L0_ffn_down_scales, dtype=np.float16).T)
    ic_cnt = [6, 6, 6, 6, 5, 5, 5, 5]
    ic_start = [0, 6, 12, 18, 24, 29, 34, 39]
    in_maps = []
    for c in range(NCORE):
        sel = np.zeros((1, 2), np.float32)
        sel[0, 0 if c < 4 else 1] = 1.0
        s0, cnt = ic_start[c], ic_cnt[c]
        dnp_c = np.zeros((D, 96), np.int32)
        dnp_c[:, 0:cnt * 16] = dnp[:, s0 * 16:(s0 + cnt) * 16]
        dns_c = np.zeros((D, 6), np.float16)
        dns_c[:, 0:cnt] = dns[:, s0:s0 + cnt]
        in_maps.append({
            "x_in": x[c * T:(c + 1) * T],
            "gup": gup[c * 1408:(c + 1) * 1408],
            "gus": gus[c * 1408:(c + 1) * 1408],
            "dnp": np.ascontiguousarray(dnp_c),
            "dns": np.ascontiguousarray(dns_c),
            "wq": np.ascontiguousarray(wq[:, c * 256:(c + 1) * 256]),
            "wk": np.ascontiguousarray(wk[:, c * 256:(c + 1) * 256]),
            "wv": np.ascontiguousarray(wv[:, c * 256:(c + 1) * 256]),
            "wo": np.ascontiguousarray(wo[c * 256:(c + 1) * 256]),
            "bo": bo,
            "selb": sel,
        })
    return in_maps


_NC_CACHE = None


def kernel(L0_attn_hidden_states, L0_attn_Wq, L0_attn_Wk, L0_attn_Wv, L0_attn_Wo,
           L0_attn_bo, L0_ffn_gate_up_packed, L0_ffn_gate_up_scales,
           L0_ffn_down_packed, L0_ffn_down_scales):
    global _NC_CACHE
    in_maps = make_in_maps(L0_attn_hidden_states, L0_attn_Wq, L0_attn_Wk, L0_attn_Wv,
                           L0_attn_Wo, L0_attn_bo, L0_ffn_gate_up_packed,
                           L0_ffn_gate_up_scales, L0_ffn_down_packed,
                           L0_ffn_down_scales)
    if _NC_CACHE is None:
        _NC_CACHE = build_program()
    nc = _NC_CACHE
    res = run_bass_kernel_spmd(nc, in_maps, core_ids=list(range(NCORE)))
    out = np.concatenate([r["y"] for r in res.results], axis=0)
    return out.reshape(2, 4096, 2048)


# version 31 (final)
# speedup vs baseline: 1.0252x; 1.0039x over previous
import numpy as np
import concourse.bass as bass
import concourse.bass_utils as _bu
import concourse.mybir as mybir
from concourse.bass_utils import run_bass_kernel_spmd

if not getattr(_bu, "_ldw_patched", False):
    _orig_run_command = _bu.run_command

    def _run_command_ldw(argv, **kwargs):
        argv = ["--enable-ldw-opt=true" if a == "--enable-ldw-opt=false" else a
                for a in argv]
        return _orig_run_command(argv, **kwargs)

    _bu.run_command = _run_command_ldw
    _bu._ldw_patched = True

F32 = mybir.dt.float32
BF16 = mybir.dt.bfloat16
F16 = mybir.dt.float16
I32 = mybir.dt.int32
AF = mybir.ActivationFunctionType
ALU = mybir.AluOpType
AX = mybir.AxisListType

NCORE = 8
D = 2048
KC = D // 128
T = 1024
NT = T // 128
IFF = 5632
OCG = 2 * IFF // 128
ICD = IFF // 128
NH2 = 16
EPS_LN = 1e-5
EPS_RMS = 1e-6
CLIP = 1e-6

DQ_SUB_I32 = 128
GU_SPLITS = [(0, 2), (2, 4), (4, 6), (6, 8), (8, 11)]


class Buf:
    __slots__ = ("name", "wtok", "rtoks")

    def __init__(self, name):
        self.name = name
        self.wtok = None
        self.rtoks = {}


class Ctx:
    def __init__(self, nc):
        self.nc = nc
        self.sems = {}
        self.counts = {}
        names = ["pe", "dv", "sc", "gp", "qcc0", "qcc1"]
        names += [f"qsy{i}" for i in range(8)] + [f"qgp{i}" for i in range(8)]
        names += [f"qsc{i}" for i in range(8)]
        for s in names:
            self.sems[s] = nc.alloc_semaphore(s)
            self.counts[s] = 0
        self.qrr = {"qsy": 0, "qgp": 0, "qsc": 0, "qcc": 0}
        self.streams = {"pe": [], "dv": [], "sc": [], "gp": [], "sy": []}
        self.marks = {e: {} for e in self.streams}
        self.eng_sem = {"pe": "pe", "dv": "dv", "sc": "sc", "gp": "gp", "sy": None}

    def _wait(self, stream, sem, val):
        if val <= 0:
            return
        if self.marks[stream].get(sem, 0) >= val:
            return
        self.marks[stream][sem] = val
        s = self.sems[sem]
        self.streams[stream].append(lambda eng, s=s, val=val: eng.wait_ge(s, val))

    def op(self, stream, fn, reads=(), writes=(), sem=None, inc=0, raw_deps=()):
        need = {}

        def add(tok):
            if tok is None:
                return
            s, v = tok
            if need.get(s, 0) < v:
                need[s] = v

        for b in reads:
            add(b.wtok)
        for b in writes:
            add(b.wtok)
            for s, v in b.rtoks.items():
                add((s, v))
        for tok in raw_deps:
            add(tok)
        for s, v in sorted(need.items()):
            self._wait(stream, s, v)

        tok = None
        if sem is not None and inc:
            self.counts[sem] += inc
            tok = (sem, self.counts[sem])
            semh = self.sems[sem]

            def thunk(eng, fn=fn, semh=semh, cnt=inc):
                fn(eng).then_inc(semh, cnt)
            self.streams[stream].append(thunk)
        else:
            self.streams[stream].append(lambda eng, fn=fn: fn(eng))

        if tok is not None:
            for b in writes:
                b.wtok = tok
                b.rtoks = {}
            for b in reads:
                s, v = tok
                if b.rtoks.get(s, 0) < v:
                    b.rtoks[s] = v
        return tok

    def barrier(self):
        snap = dict(self.counts)
        for stream in self.streams:
            own = self.eng_sem[stream]
            for s, v in snap.items():
                if s != own:
                    self._wait(stream, s, v)

    def emit(self):
        with self.nc.Block() as block:
            block.sync(lambda eng: [t(eng) for t in self.streams["sy"]])
            block.tensor(lambda eng: [t(eng) for t in self.streams["pe"]])
            block.vector(lambda eng: [t(eng) for t in self.streams["dv"]])
            block.scalar(lambda eng: [t(eng) for t in self.streams["sc"]])
            block.gpsimd(lambda eng: [t(eng) for t in self.streams["gp"]])


class Arena:
    def __init__(self, nc, total_bytes):
        self.t = nc.alloc_sbuf_tensor("arena", [128, total_bytes // 4], F32)
        self.base = self.t.ap()
        self.off = 0
        self.total = total_bytes
        self.stack = []
        self.peak = 0

    def push(self):
        self.stack.append(self.off)

    def pop(self):
        self.off = self.stack.pop()

    def alloc(self, cols, dtype, rows=128):
        esz = {F32: 4, I32: 4, BF16: 2, F16: 2}[dtype]
        self.off = (self.off + 3) // 4 * 4
        start = self.off
        self.off += cols * esz
        self.peak = max(self.peak, self.off)
        assert self.off <= self.total, f"arena overflow: {self.off} > {self.total}"
        c0 = start // 4
        if dtype is F32:
            ap = self.base[:, c0:c0 + cols]
        elif dtype is I32:
            ap = self.base[:, c0:c0 + cols].bitcast(I32)
        else:
            n4 = (cols + 1) // 2
            ap = self.base[:, c0:c0 + n4].bitcast(dtype)[:, 0:cols]
        if rows != 128:
            ap = ap[0:rows, :]
        return ap


def build_program(phases=("ffn1", "attn", "ffn3"), dumps=()):
    nc = bass.Bass("TRN2", target_bir_lowering=False, num_devices=NCORE)

    x_in = nc.dram_tensor("x_in", [T, D], F32, kind="ExternalInput").ap()
    gup_in = nc.dram_tensor("gup", [1408, 256], I32, kind="ExternalInput").ap()
    gus_in = nc.dram_tensor("gus", [1408, 16], F16, kind="ExternalInput").ap()
    dnp_in = nc.dram_tensor("dnp", [D, 96], I32, kind="ExternalInput").ap()
    dns_in = nc.dram_tensor("dns", [D, 6], F16, kind="ExternalInput").ap()
    wq_in = nc.dram_tensor("wq", [D, 256], F32, kind="ExternalInput").ap()
    wk_in = nc.dram_tensor("wk", [D, 256], F32, kind="ExternalInput").ap()
    wv_in = nc.dram_tensor("wv", [D, 256], F32, kind="ExternalInput").ap()
    wo_in = nc.dram_tensor("wo", [256, D], F32, kind="ExternalInput").ap()
    bo_in = nc.dram_tensor("bo", [1, D], F32, kind="ExternalInput").ap()
    selb_in = nc.dram_tensor("selb", [1, 2], F32, kind="ExternalInput").ap()
    y_out = nc.dram_tensor("y", [T, D], F32, kind="ExternalOutput").ap()

    wsh_all = nc.dram_tensor("wsh_all", [8, 262144], BF16, kind="Internal").ap()
    W_IDX = {"wqt": 0, "wkt": 1, "wvt": 2, "wot": 3}
    wsh = {w: wsh_all[W_IDX[w] * 2:W_IDX[w] * 2 + 2, :]
           for w in ("wqt", "wkt", "wvt", "wot")}
    gu_sh = [nc.dram_tensor(f"gu_sh{i}", [(b - a), 262144], BF16, kind="Internal").ap()
             for i, (a, b) in enumerate(GU_SPLITS)]
    gu_all = [nc.dram_tensor(f"gu_all{i}", [8 * (b - a), 262144], BF16,
                             kind="Internal", addr_space="Shared").ap()
              for i, (a, b) in enumerate(GU_SPLITS)]
    w_lin_all = nc.dram_tensor("w_lin_all", [64, 262144], BF16, kind="Internal",
                               addr_space="Shared").ap()

    def w_lin_row(w, mc):
        widx = {"wqt": 0, "wkt": 1, "wvt": 2, "wot": 3}[w]
        return (mc // 2) * 8 + widx * 2 + (mc % 2)
    dn_sh = nc.dram_tensor("dn_sh", [6, 262144], BF16, kind="Internal").ap()
    dn_all = nc.dram_tensor("dn_all", [48, 262144], BF16, kind="Internal",
                            addr_space="Shared").ap()
    kv_cci = nc.dram_tensor("kv_cci", [2 * 2048, 64], F32, kind="Internal").ap()
    kv_cco = nc.dram_tensor("kv_cco", [2 * 2048, 64], F32, kind="Internal").ap()
    ks_cci = nc.dram_tensor("ks_cci", [2 * 128, 16], F32, kind="Internal").ap()
    ks_cco = nc.dram_tensor("ks_cco", [2 * 128, 16], F32, kind="Internal").ap()

    ctx = Ctx(nc)
    ar = Arena(nc, 207360)

    x_tm = [ar.alloc(D, F32) for _ in range(NT)]
    xb = [Buf(f"x{j}") for j in range(NT)]
    ident = ar.alloc(128, BF16)
    ident_b = Buf("ident")
    ones_row = ar.alloc(128, BF16)
    ones_b = Buf("ones")
    bo_bf = ar.alloc(D, BF16)
    bo_bb = Buf("bo_bf")
    selb = ar.alloc(2, F32)
    selb_b = Buf("selb")
    selb_bf = ar.alloc(2, BF16)
    selb_bfb = Buf("selb_bf")
    gus_sc = ar.alloc(16 * 11, F32)
    gus_b = Buf("gus_sc")
    dns_sc = ar.alloc(16 * 6, F32)
    dns_b = Buf("dns_sc")
    trash = ar.alloc(1024, BF16)
    trash_b = Buf("trash")
    statc = ar.alloc(16, F32)
    stat_b = Buf("stat")

    psum = nc.alloc_psum_tensor("psum", [128, 4096], F32).ap()
    pbank = [Buf(f"pb{k}") for k in range(8)]

    def pbf(bank):
        return psum[:, bank * 512:(bank + 1) * 512].bitcast(BF16)

    def coll(fn, reads=(), writes=()):
        lane = ctx.qrr["qcc"]
        ctx.qrr["qcc"] = (lane + 1) % 2
        sem = f"qcc{lane}"
        prev = ctx.counts[sem]
        return ctx.op("gp", fn, reads=reads, writes=writes, sem=sem, inc=1,
                      raw_deps=[(sem, prev)])

    dump_outs = {}

    def dump(name, ap, reads):
        if name not in dumps:
            return
        shp = list(ap.shape)
        d = nc.dram_tensor("dump_" + name, shp, ap.dtype, kind="ExternalOutput").ap()
        dump_outs[name] = d
        dma("gp", d, ap, reads=reads, writes=(Buf("dmp_" + name),))

    def dma(stream, out, in_, reads=(), writes=(), raw_deps=(), transpose=False):
        q = {"sy": "qsy", "gp": "qgp", "sc": "qsc"}[stream]
        lane = ctx.qrr[q]
        ctx.qrr[q] = (lane + 1) % 8
        sem = f"{q}{lane}"
        prev = ctx.counts[sem]
        if transpose:
            fn = lambda e, o=out, i=in_: e.dma_start_transpose(out=o, in_=i)
        else:
            fn = lambda e, o=out, i=in_: e.dma_start(out=o, in_=i)
        return ctx.op(stream, fn, reads=reads, writes=writes, sem=sem, inc=16,
                      raw_deps=list(raw_deps) + [(sem, prev)])

    def vop(fn, reads=(), writes=(), stream="dv"):
        return ctx.op(stream, fn, reads=reads, writes=writes,
                      sem=ctx.eng_sem[stream], inc=1)

    def emit_ln(j, dst_tm, dst_tm_b, dst_fm, dst_fm_b, col, stg, stg_b):
        mcol = statc[:, 0:1]
        rcol = statc[:, 1:2]
        sqa = statc[:, 2:3]
        sqb = statc[:, 3:4]
        vop(lambda e, j=j, mcol=mcol: e.tensor_reduce(
            out=mcol, in_=x_tm[j], axis=AX.X, op=ALU.add),
            reads=(xb[j],), writes=(stat_b,))
        vop(lambda e, mcol=mcol: e.tensor_scalar(out=mcol, in0=mcol,
            scalar1=-1.0 / D, scalar2=None, op0=ALU.mult),
            reads=(stat_b,), writes=(stat_b,))
        ctx.op("sc", lambda e, j=j, mcol=mcol, sqa=sqa: e.activation(
            out=trash, in_=x_tm[j][:, 0:1024], func=AF.Square,
            bias=mcol, scale=1.0, accum_out=sqa),
            reads=(xb[j], stat_b), writes=(trash_b, stat_b), sem="sc", inc=1)
        ctx.op("sc", lambda e, j=j, mcol=mcol, sqb=sqb: e.activation(
            out=trash, in_=x_tm[j][:, 1024:2048], func=AF.Square,
            bias=mcol, scale=1.0, accum_out=sqb),
            reads=(xb[j], stat_b), writes=(trash_b, stat_b), sem="sc", inc=1)
        vop(lambda e, sqa=sqa, sqb=sqb: e.tensor_tensor(out=sqa, in0=sqa, in1=sqb,
            op=ALU.add), reads=(stat_b,), writes=(stat_b,))
        vop(lambda e, sqa=sqa: e.tensor_scalar(out=sqa, in0=sqa, scalar1=1.0 / D,
            scalar2=EPS_LN, op0=ALU.mult, op1=ALU.add),
            reads=(stat_b,), writes=(stat_b,))
        ctx.op("sc", lambda e, sqa=sqa, rcol=rcol: e.activation(
            out=rcol, in_=sqa, func=AF.Sqrt),
            reads=(stat_b,), writes=(stat_b,), sem="sc", inc=1)
        vop(lambda e, rcol=rcol: e.reciprocal(out=rcol, in_=rcol),
            reads=(stat_b,), writes=(stat_b,))
        vop(lambda e, j=j, mcol=mcol, rcol=rcol, dst_tm=dst_tm: e.tensor_scalar(
            out=dst_tm, in0=x_tm[j], scalar1=mcol, scalar2=rcol,
            op0=ALU.add, op1=ALU.mult),
            reads=(xb[j], stat_b), writes=(dst_tm_b,))
        for k4 in range(0, KC, 4):
            for t in range(4):
                kc = k4 + t
                ctx.op("pe", lambda e, kc=kc, t=t, dst_tm=dst_tm: e.transpose(
                    pbf(6)[:, t * 128:(t + 1) * 128],
                    dst_tm[:, kc * 128:(kc + 1) * 128], ident),
                    reads=(dst_tm_b, ident_b), writes=(pbank[6],), sem="pe", inc=1)
            vop(lambda e, stg=stg: e.tensor_copy(out=stg, in_=pbf(6)[:, 0:512]),
                reads=(pbank[6],), writes=(stg_b,))
            for t in range(4):
                kc = k4 + t
                vop(lambda e, kc=kc, t=t, col=col, dst_fm=dst_fm, stg=stg:
                    e.tensor_copy(out=dst_fm[kc][:, col:col + 128],
                                  in_=stg[:, t * 128:(t + 1) * 128]),
                    reads=(stg_b,), writes=(dst_fm_b[kc],))

    for j in range(NT):
        dma("gp", x_tm[j], x_in[j * 128:(j + 1) * 128, :], writes=(xb[j],))
    ctx.op("gp", lambda e: e.memset(ident, 0.0), writes=(ident_b,), sem="gp", inc=1)
    ctx.op("gp", lambda e: e.affine_select(out=ident, in_=ident, compare_op=ALU.not_equal,
                                           fill=1.0, base=0, pattern=[[-1, 128]],
                                           channel_multiplier=1),
           reads=(ident_b,), writes=(ident_b,), sem="gp", inc=1)
    ctx.op("gp", lambda e: e.memset(ones_row, 1.0), writes=(ones_b,), sem="gp", inc=1)
    dma("gp", selb[0:1, :], selb_in[:, :], writes=(selb_b,))
    vop(lambda e: e.tensor_copy(out=selb_bf[0:1, :], in_=selb[0:1, :]),
        reads=(selb_b,), writes=(selb_bfb,))

    ar.push()
    bo_f = ar.alloc(D, F32)
    bo_fb = Buf("bo_f")
    dma("gp", bo_f[0:1, :], bo_in[:, :], writes=(bo_fb,))
    vop(lambda e: e.tensor_copy(out=bo_bf[0:1, :], in_=bo_f[0:1, :]),
        reads=(bo_fb,), writes=(bo_bb,))
    sc_f16 = ar.alloc(16 * 11, F16)
    sc_b = Buf("sc_f16")
    dma("gp", sc_f16.rearrange("p (j g) -> p j g", g=16),
        gus_in.rearrange("(j p) g -> p j g", p=128), writes=(sc_b,))
    vop(lambda e: e.tensor_copy(out=gus_sc, in_=sc_f16), reads=(sc_b,), writes=(gus_b,))
    scd_f16 = ar.alloc(16 * 6, F16)
    scd_b = Buf("scd_f16")
    dma("gp", scd_f16.rearrange("p (j g) -> p j g", g=6),
        dns_in.rearrange("(j p) g -> p j g", p=128), writes=(scd_b,))
    vop(lambda e: e.tensor_copy(out=dns_sc, in_=scd_f16), reads=(scd_b,), writes=(dns_b,))

    ar.pop()

    dn_lin_b = Buf("dn_lin")
    attn_lin_b = Buf("attn_lin")
    wshb = {w: Buf(f"{w}shb") for w in ("wqt", "wkt", "wvt", "wot")}

    def conv_jobs(cva):
        wcv_f, wcv_fb, wcv_bf, wcv_bb = cva
        jobs = []
        for wname, win in (("wqt", wq_in), ("wkt", wk_in), ("wvt", wv_in)):
            for s in range(2):
                def cj(wname=wname, win=win, s=s):
                    dma("sc", wcv_f.rearrange("p (k m) -> p k m", m=128),
                        win.rearrange("(k p) m -> p k m", p=128)
                        [:, :, s * 128:(s + 1) * 128], writes=(wcv_fb,))
                    vop(lambda e, wcv_bf=wcv_bf, wcv_f=wcv_f:
                        e.tensor_copy(out=wcv_bf, in_=wcv_f),
                        reads=(wcv_fb,), writes=(wcv_bb,))
                    dma("sc", wsh[wname].rearrange("n (p f) -> n p f", p=128)[s],
                        wcv_bf, reads=(wcv_bb,), writes=(wshb[wname],))
                jobs.append(cj)
        for s in range(2):
            def cjo(s=s):
                dma("sc", wcv_f, wo_in[s * 128:(s + 1) * 128, :], writes=(wcv_fb,))
                vop(lambda e, wcv_bf=wcv_bf, wcv_f=wcv_f:
                    e.tensor_copy(out=wcv_bf, in_=wcv_f),
                    reads=(wcv_fb,), writes=(wcv_bb,))
                dma("sc", wsh["wot"].rearrange("n (p f) -> n p f", p=128)[s], wcv_bf,
                    reads=(wcv_bb,), writes=(wshb["wot"],))
            jobs.append(cjo)
        jobs.append(lambda: coll(lambda e: e.collective_compute(
            "AllGather", ALU.bypass, replica_groups=[list(range(NCORE))],
            ins=[wsh_all.tensor.ap().opt()], outs=[w_lin_all.tensor.ap().opt()]),
            reads=tuple(wshb.values()), writes=(attn_lin_b,)))
        return jobs

    def emit_dequant_dve(packed_ap, row_off, n_i32, scale_cols, scale_b,
                         scale_col_off, dqa, pb):
        w_t = dqa["w"]
        nib = dqa["nib"]
        lo3 = dqa["lo3"]
        sgn = dqa["sgn"]
        bits = dqa["bits"]
        lo3f = dqa["lo3f"]
        corr = dqa["corr"]
        dq_bf = dqa["dqbf"][pb]
        dqbfb = dqa["dqbfb"][pb]
        dma("sy", w_t[:, 0:n_i32], packed_ap[row_off:row_off + 128, 0:n_i32],
            writes=(dqa["wb"],))
        n_sub = (n_i32 + DQ_SUB_I32 - 1) // DQ_SUB_I32
        for s in range(n_sub):
            c0 = s * DQ_SUB_I32
            c1 = min(n_i32, c0 + DQ_SUB_I32)
            nw = (c1 - c0) * 8
            o0 = c0 * 8
            nibv = nib[:, 0:nw].rearrange("p (g e) -> p g e", e=8)
            for jn in range(8):
                vop(lambda e, jn=jn, c0=c0, c1=c1, nibv=nibv:
                    e.tensor_scalar(out=nibv[:, :, jn], in0=w_t[:, c0:c1],
                                    scalar1=4 * jn, scalar2=15,
                                    op0=ALU.logical_shift_right, op1=ALU.bitwise_and),
                    reads=(dqa["wb"],), writes=(dqa["nibb"],))
            vop(lambda e, nw=nw: e.tensor_scalar(out=lo3[:, 0:nw], in0=nib[:, 0:nw],
                scalar1=7, scalar2=None, op0=ALU.bitwise_and),
                reads=(dqa["nibb"],), writes=(dqa["lo3b"],))
            vop(lambda e, nw=nw: e.tensor_scalar(out=sgn[:, 0:nw], in0=nib[:, 0:nw],
                scalar1=8, scalar2=28, op0=ALU.bitwise_and, op1=ALU.logical_shift_left),
                reads=(dqa["nibb"],), writes=(dqa["sgnb"],))
            vop(lambda e, nw=nw: e.tensor_scalar(out=bits[:, 0:nw], in0=lo3[:, 0:nw],
                scalar1=0xFC, scalar2=(1 << 22), op0=ALU.add, op1=ALU.mult),
                reads=(dqa["lo3b"],), writes=(dqa["bitsb"],))
            vop(lambda e, nw=nw: e.tensor_scalar(out=lo3f[:, 0:nw],
                in0=bits[:, 0:nw].bitcast(F32), scalar1=-1.0, scalar2=0.0,
                op0=ALU.add, op1=ALU.min),
                reads=(dqa["bitsb"],), writes=(dqa["lo3fb"],))
            vop(lambda e, nw=nw: e.tensor_tensor(out=corr[:, 0:nw],
                in0=bits[:, 0:nw].bitcast(F32), in1=lo3f[:, 0:nw], op=ALU.add),
                reads=(dqa["bitsb"], dqa["lo3fb"]), writes=(dqa["corrb"],),
                stream="gp")
            vop(lambda e, nw=nw: e.tensor_tensor(out=lo3f[:, 0:nw].bitcast(I32),
                in0=corr[:, 0:nw].bitcast(I32), in1=sgn[:, 0:nw], op=ALU.bitwise_or),
                reads=(dqa["corrb"], dqa["sgnb"]), writes=(dqa["lo3fb"],))
            g0 = o0 // 128
            for g in range(nw // 128):
                vop(lambda e, g=g, o0=o0, g0=g0, scale_cols=scale_cols:
                    e.tensor_scalar(out=dq_bf[:, o0 + g * 128:o0 + (g + 1) * 128],
                                    in0=lo3f[:, g * 128:(g + 1) * 128],
                                    scalar1=scale_cols[:, scale_col_off + g0 + g:
                                                       scale_col_off + g0 + g + 1],
                                    scalar2=None, op0=ALU.mult),
                    reads=(dqa["lo3fb"], scale_b), writes=(dqbfb,))

    def emit_dequant_tail(n_i32, tile_outs, dqa, pb):
        dq_bf = dqa["dqbf"][pb]
        dqbfb = dqa["dqbfb"][pb]
        stg = dqa["stg"]
        n_kc = n_i32 * 8 // 128
        for kc4 in range(0, n_kc, 4):
            kn = min(4, n_kc - kc4)
            for t in range(kn):
                kc = kc4 + t
                ctx.op("pe", lambda e, kc=kc, t=t, dq_bf=dq_bf: e.transpose(
                    pbf(7)[:, t * 128:(t + 1) * 128],
                    dq_bf[:, kc * 128:(kc + 1) * 128], ident),
                    reads=(dqbfb, ident_b), writes=(pbank[7],), sem="pe", inc=1)
            vop(lambda e, kn=kn, stg=stg: e.copy(
                out=stg[:, 0:kn * 128], in_=pbf(7)[:, 0:kn * 128]),
                reads=(pbank[7],), writes=(dqa["stgb"],), stream="sc")
            contig = kn > 1 and all(tile_outs[kc4 + t] is None for t in range(1, kn))
            if contig:
                dma("sy", tile_outs[kc4], stg[:, 0:kn * 128],
                    reads=(dqa["stgb"],), writes=(dqa["shb"],))
            else:
                for t in range(kn):
                    dma("sy", tile_outs[kc4 + t], stg[:, t * 128:(t + 1) * 128],
                        reads=(dqa["stgb"],), writes=(dqa["shb"],))

    def dq_alloc():
        dqa = {"shb": Buf("dqsh")}
        dqa["w"] = ar.alloc(704, I32)
        dqa["wb"] = Buf("dqw")
        nw = DQ_SUB_I32 * 8
        for nm in ("nib", "lo3", "sgn", "bits"):
            dqa[nm] = ar.alloc(nw, I32)
            dqa[nm + "b"] = Buf(nm)
        dqa["lo3f"] = ar.alloc(nw, F32)
        dqa["lo3fb"] = Buf("lo3f")
        dqa["corr"] = ar.alloc(nw, F32)
        dqa["corrb"] = Buf("corr")
        dqa["dqbf"] = [ar.alloc(IFF, BF16), ar.alloc(IFF, BF16)]
        dqa["dqbfb"] = [Buf("dqbf0"), Buf("dqbf1")]
        dqa["stg"] = ar.alloc(512, BF16)
        dqa["stgb"] = Buf("dqstg")
        return dqa

    gu_gather_b = [Buf(f"gugather{i}") for i in range(len(GU_SPLITS))]

    def dq_jobs(dqa):
        chunks = []
        for i, (a, b) in enumerate(GU_SPLITS):
            for j in range(a, b):
                def gup_dve(pb, i=i, j=j):
                    emit_dequant_dve(gup_in, j * 128, 256, gus_sc, gus_b, j * 16,
                                     dqa, pb)

                def gup_tail(pb, i=i, j=j, a=a):
                    panel = gu_sh[i].rearrange("n (p f) -> n p f", p=128)[j - a]
                    touts = []
                    for kc4 in range(0, 16, 4):
                        touts.append(panel[:, kc4 * 128:(kc4 + 4) * 128])
                        touts += [None, None, None]
                    emit_dequant_tail(256, touts, dqa, pb)
                chunks.append((gup_dve, gup_tail, None))
            chunks[-1] = (chunks[-1][0], chunks[-1][1],
                          lambda i=i: coll(lambda e, i=i: e.collective_compute(
                              "AllGather", ALU.bypass,
                              replica_groups=[list(range(NCORE))],
                              ins=[gu_sh[i].tensor.ap().opt()],
                              outs=[gu_all[i].tensor.ap().opt()]),
                              reads=(dqa["shb"],), writes=(gu_gather_b[i],)))
        dn_panels = dn_sh.rearrange("n (p f) -> n p f", p=128)
        dn_chunks = []
        for fk in range(16):
            def dn_dve(pb, fk=fk):
                emit_dequant_dve(dnp_in, fk * 128, 96, dns_sc, dns_b, fk * 6,
                                 dqa, pb)

            def dn_tail(pb, fk=fk):
                touts = [dn_panels[s][:, fk * 128:(fk + 1) * 128] for s in range(6)]
                emit_dequant_tail(96, touts, dqa, pb)
            fin = None
            if fk == 15:
                def fin():
                    coll(lambda e: e.collective_compute(
                        "AllGather", ALU.bypass, replica_groups=[list(range(NCORE))],
                        ins=[dn_sh.tensor.ap().opt()], outs=[dn_all.tensor.ap().opt()]),
                        reads=(dqa["shb"],), writes=(dn_lin_b,))
            dn_chunks.append((dn_dve, dn_tail, fin))
        head, rest = chunks[:4], chunks[4:]
        merged = list(head)
        di = 0
        for ch in rest:
            for _ in range(3):
                if di < len(dn_chunks):
                    merged.append(dn_chunks[di])
                    di += 1
            merged.append(ch)
        merged.extend(dn_chunks[di:])
        chunks = merged
        jobs = []
        for k in range(len(chunks) + 1):
            def job(k=k):
                if k < len(chunks):
                    chunks[k][0](k % 2)
                if k > 0:
                    chunks[k - 1][1]((k - 1) % 2)
                    if chunks[k - 1][2] is not None:
                        chunks[k - 1][2]()
            jobs.append(job)
        return jobs

    def emit_ffn(segs, dodump=False, with_jobs=False, nwg=3):
        ntt = 4
        TW = ntt * 128
        nb = TW // 512
        ar.push()
        dqa = dq_alloc() if with_jobs else None
        cva = None
        if with_jobs:
            cva = (dqa["dqbf"][0].bitcast(F32)[:, 0:D], dqa["dqbfb"][0],
                   dqa["dqbf"][1][:, 0:D], dqa["dqbfb"][1])
        xn_fm = [ar.alloc(TW, BF16) for _ in range(KC)]
        xnb = [Buf(f"xn{k}") for k in range(KC)]
        h_fm = [ar.alloc(TW, BF16) for _ in range(ICD)]
        hb = [Buf(f"h{i}") for i in range(ICD)]
        xn_tm = ar.alloc(D, BF16)
        xn_tm_b = Buf("xn_tm")
        NWG = nwg
        wg_sb = [ar.alloc(D, BF16) for _ in range(NWG)]
        wg_b = [Buf(f"wg{i}") for i in range(NWG)]
        sg_sb = [ar.alloc(TW, F32) for _ in range(2)]
        sg_b = [Buf(f"sg{i}") for i in range(2)]
        stg = ar.alloc(512, BF16)
        stg_b = Buf("tstg")

        for si, tok0 in enumerate(segs):
            seg_jobs = []
            if with_jobs:
                seg_jobs = dq_jobs(dqa) if si == 0 else conv_jobs(cva)
            emit_ffn_seg(si, tok0, seg_jobs, dodump and si == 0 and tok0 == 0,
                         TW, nb, xn_fm, xnb, h_fm, hb, xn_tm, xn_tm_b,
                         NWG, wg_sb, wg_b, sg_sb, sg_b, stg, stg_b)
        ar.pop()

    def emit_ffn_seg(si, tok0, jobs, dodump, TW, nb, xn_fm, xnb, h_fm, hb,
                     xn_tm, xn_tm_b, NWG, wg_sb, wg_b, sg_sb, sg_b, stg, stg_b):
        ntt = TW // 128
        if dodump:
            dump("x0", x_tm[0], (xb[0],))
        for jt in range(ntt):
            emit_ln(tok0 // 128 + jt, xn_tm, xn_tm_b, xn_fm, xnb, jt * 128, stg, stg_b)
            if dodump and jt == 0:
                dump("xn0", xn_tm, (xn_tm_b,))
                dump("st0", statc, (stat_b,))
        if dodump:
            dump("xnf0", xn_fm[0], (xnb[0],))

        ji = 0
        if si == 0 and jobs:
            na, nb_ = GU_SPLITS[0]
            while ji <= nb_ - na:
                jobs[ji]()
                ji += 1
        pair_seq = [(i, c, j) for i, (a, b) in enumerate(GU_SPLITS)
                    for c in range(4) for j in range(a, b)]
        for pi, (i, c, j) in enumerate(pair_seq):
            for _ in range(2):
                if ji < len(jobs):
                    jobs[ji]()
                    ji += 1
            a, b = GU_SPLITS[i]
            oc = c * 11 + j
            for half in range(2):
                cc = c + 4 * half
                buf = (2 * pi + half) % NWG
                r0 = cc * (b - a) + (j - a)
                dma("sy" if half == 0 else "gp", wg_sb[buf],
                    gu_all[i].rearrange("n (p f) -> n p f", p=128)[r0],
                    reads=(gu_gather_b[i],), writes=(wg_b[buf],))
                bank0 = (0 if half == 0 else nb) + (0 if pi % 2 == 0 else 2 * nb)
                pw = [pbank[bank0 + q] for q in range(nb)]
                for kc in range(KC):
                    for q in range(nb):
                        ctx.op("pe", lambda e, kc=kc, q=q, buf=buf, bank0=bank0,
                               wg_sb=wg_sb, xn_fm=xn_fm:
                               e.matmul(psum[:, (bank0 + q) * 512:(bank0 + q + 1) * 512],
                                        wg_sb[buf][:, kc * 128:(kc + 1) * 128],
                                        xn_fm[kc][:, q * 512:(q + 1) * 512],
                                        start=(kc == 0), stop=(kc == KC - 1)),
                               reads=(wg_b[buf], xnb[kc]),
                               writes=tuple(pw) if kc in (0, KC - 1) else (),
                               sem="pe", inc=1 if kc == KC - 1 else 0)
                if half == 0:
                    sbuf = pi % 2
                    ctx.op("sc", lambda e, sbuf=sbuf, bank0=bank0, nb=nb, sg_sb=sg_sb:
                           e.activation(out=sg_sb[sbuf],
                                        in_=psum[:, bank0 * 512:(bank0 + nb) * 512],
                                        func=AF.Silu),
                           reads=tuple(pw), writes=(sg_b[sbuf],), sem="sc", inc=1)
                else:
                    sbuf = pi % 2
                    vop(lambda e, sbuf=sbuf, oc=oc, bank0=bank0, nb=nb, sg_sb=sg_sb,
                        h_fm=h_fm: e.tensor_tensor(
                            out=h_fm[oc], in0=sg_sb[sbuf],
                            in1=psum[:, bank0 * 512:(bank0 + nb) * 512], op=ALU.mult),
                        reads=(sg_b[sbuf],) + tuple(pw), writes=(hb[oc],))

        while ji < len(jobs):
            jobs[ji]()
            ji += 1
        if dodump:
            dump("h0", h_fm[0], (hb[0],))
            dump("h43", h_fm[43], (hb[43],))
        for tp in range(ntt // 2):
            pdn = ([pbank[q] for q in range(4)], [pbank[4 + q] for q in range(4)])
            for ic in range(ICD):
                buf = ic % NWG
                if ic < 24:
                    row = (ic // 6) * 6 + ic % 6
                else:
                    row = 24 + ((ic - 24) // 5) * 6 + (ic - 24) % 5
                dma("sy" if ic % 2 == 0 else "gp", wg_sb[buf],
                    dn_all.rearrange("n (p f) -> n p f", p=128)[row],
                    reads=(dn_lin_b,), writes=(wg_b[buf],))
                for half in range(2):
                    jt = tp * 2 + half
                    for q in range(4):
                        last = (half == 1 and q == 3)
                        ctx.op("pe", lambda e, ic=ic, buf=buf, half=half, jt=jt, q=q,
                               wg_sb=wg_sb, h_fm=h_fm:
                               e.matmul(psum[:, (half * 4 + q) * 512:
                                              (half * 4 + q + 1) * 512],
                                        h_fm[ic][:, jt * 128:(jt + 1) * 128],
                                        wg_sb[buf][:, q * 512:(q + 1) * 512],
                                        start=(ic == 0), stop=(ic == ICD - 1)),
                               reads=(wg_b[buf], hb[ic]),
                               writes=(tuple(pdn[0]) + tuple(pdn[1]))
                               if ((ic == 0 and half == 0 and q == 0) or
                                   (ic == ICD - 1 and last)) else (),
                               sem="pe", inc=1 if last else 0)
            for half in range(2):
                jt = tp * 2 + half
                j = tok0 // 128 + jt
                pw = pdn[half]
                sqa = statc[:, 4:5]
                sqb = statc[:, 5:6]
                rr = statc[:, 6:7]
                ctx.op("sc", lambda e, half=half, sqa=sqa: e.activation(
                    out=trash, in_=psum[:, half * 2048:half * 2048 + 1024],
                    func=AF.Square, accum_out=sqa),
                    reads=tuple(pw) + (trash_b,), writes=(trash_b, stat_b),
                    sem="sc", inc=1)
                ctx.op("sc", lambda e, half=half, sqb=sqb: e.activation(
                    out=trash, in_=psum[:, half * 2048 + 1024:half * 2048 + 2048],
                    func=AF.Square, accum_out=sqb),
                    reads=tuple(pw) + (trash_b,), writes=(trash_b, stat_b),
                    sem="sc", inc=1)
                vop(lambda e, sqa=sqa, sqb=sqb: e.tensor_tensor(out=sqa, in0=sqa,
                    in1=sqb, op=ALU.add), reads=(stat_b,), writes=(stat_b,))
                vop(lambda e, sqa=sqa: e.tensor_scalar(out=sqa, in0=sqa,
                    scalar1=1.0 / D, scalar2=EPS_RMS, op0=ALU.mult, op1=ALU.add),
                    reads=(stat_b,), writes=(stat_b,))
                ctx.op("sc", lambda e, sqa=sqa, rr=rr: e.activation(
                    out=rr, in_=sqa, func=AF.Sqrt),
                    reads=(stat_b,), writes=(stat_b,), sem="sc", inc=1)
                vop(lambda e, rr=rr: e.reciprocal(out=rr, in_=rr),
                    reads=(stat_b,), writes=(stat_b,))
                vop(lambda e, rr=rr: e.tensor_scalar(out=rr, in0=rr, scalar1=0.5,
                    scalar2=None, op0=ALU.mult), reads=(stat_b,), writes=(stat_b,))
                vop(lambda e, half=half, j=j, rr=rr: e.scalar_tensor_tensor(
                    out=x_tm[j], in0=psum[:, half * 2048:(half + 1) * 2048],
                    scalar=rr, in1=x_tm[j], op0=ALU.mult, op1=ALU.add),
                    reads=(stat_b, xb[j]) + tuple(pw), writes=(xb[j],))

    def emit_attn():
        ar.push()
        qkvn_fm = [ar.alloc(T, BF16) for _ in range(KC)]
        qkvn_fmb = [Buf(f"qkvnf{k}") for k in range(KC)]
        kvbd = [ar.alloc(128, BF16) for _ in range(NH2)]
        kvbd_b = [Buf(f"kvbd{c}") for c in range(NH2)]
        kscols = ar.alloc(32, BF16)
        kscols_b = Buf("kscols")
        qkvn = ar.alloc(D, BF16)
        qkvn_b = Buf("qkvn")
        rn = ar.alloc(32, F32)
        rn_b = Buf("rn")
        selc = ar.alloc(2, F32)
        selc_b = Buf("selc")

        ar.push()
        xn2_fm = [ar.alloc(T, BF16) for _ in range(KC)]
        xn2b = [Buf(f"xn2{k}") for k in range(KC)]
        ar.push()
        xn_tm2 = ar.alloc(D, BF16)
        xn_tm2_b = Buf("xn_tm2")
        stg2 = ar.alloc(512, BF16)
        stg2_b = Buf("stg2")
        for j in range(NT):
            emit_ln(j, xn_tm2, xn_tm2_b, xn2_fm, xn2b, j * 128, stg2, stg2_b)
        ar.pop()
        ctx.barrier()

        q_fm = [ar.alloc(T, BF16) for _ in range(KC)]
        qfb = [Buf(f"qf{m}") for m in range(KC)]
        kv_ws = [ar.alloc(D, BF16) for _ in range(2)]
        kv_wsb = [Buf(f"kvws{i}") for i in range(2)]

        def wtile_dma(wname, mc, buf):
            dma("sy", kv_ws[buf],
                w_lin_all.rearrange("n (p f) -> n p f", p=128)[w_lin_row(wname, mc)],
                reads=(attn_lin_b,), writes=(kv_wsb[buf],))

        ctx.op("pe", lambda e: e.matmul(psum[:, 7 * 512:7 * 512 + 2],
                                        ones_row[0:1, :], selb_bf[0:1, :],
                                        start=True, stop=True),
               reads=(ones_b, selb_bfb), writes=(pbank[7],), sem="pe", inc=1)
        vop(lambda e: e.tensor_copy(out=selc, in_=psum[:, 7 * 512:7 * 512 + 2]),
            reads=(pbank[7],), writes=(selc_b,))

        ar.push()
        kc_sb = [ar.alloc(T, BF16) for _ in range(2)]
        kc_b = [Buf(f"kc{i}") for i in range(2)]
        vc_sb = [ar.alloc(T, BF16) for _ in range(2)]
        vc_b = [Buf(f"vc{i}") for i in range(2)]
        ks_sb = ar.alloc(16, F32)
        ks_b = Buf("ks")
        kvps = [pbank[4 + q] for q in range(4)]
        NTMP = 4
        ktmp = [ar.alloc(128, BF16) for _ in range(NTMP)]
        ktmp_b = [Buf(f"ktmp{i}") for i in range(NTMP)]
        vtmp = [ar.alloc(128, BF16) for _ in range(NTMP)]
        vtmp_b = [Buf(f"vtmp{i}") for i in range(NTMP)]

        def kv_prod(c16):
            pr = c16 % 2
            kb0 = 0 if pr == 0 else 2
            vb0 = 2 if pr == 0 else 0
            for wi, (wname, dst, dstb, actf, bank0) in enumerate(
                    (("wkt", kc_sb[pr], kc_b[pr], AF.Relu, kb0),
                     ("wvt", vc_sb[pr], vc_b[pr], AF.Identity, vb0))):
                buf = (2 * c16 + wi) % 2
                wtile_dma(wname, c16, buf)
                pw = [pbank[bank0], pbank[bank0 + 1]]
                for kc in range(KC):
                    for q in range(2):
                        ctx.op("pe", lambda e, kc=kc, q=q, buf=buf, bank0=bank0,
                               xn2_fm=xn2_fm:
                               e.matmul(psum[:, (bank0 + q) * 512:(bank0 + q + 1) * 512],
                                        kv_ws[buf][:, kc * 128:(kc + 1) * 128],
                                        xn2_fm[kc][:, q * 512:(q + 1) * 512],
                                        start=(kc == 0), stop=(kc == KC - 1)),
                               reads=(kv_wsb[buf], xn2b[kc]),
                               writes=tuple(pw) if kc in (0, KC - 1) else (),
                               sem="pe", inc=1 if kc == KC - 1 else 0)
                ctx.op("sc", lambda e, dst=dst, actf=actf, bank0=bank0: e.activation(
                    out=dst, in_=psum[:, bank0 * 512:(bank0 + 2) * 512], func=actf),
                    reads=tuple(pw), writes=(dstb,), sem="sc", inc=1)
            vop(lambda e, pr=pr, c16=c16: e.tensor_reduce(
                out=ks_sb[:, c16:c16 + 1], in_=kc_sb[pr], axis=AX.X, op=ALU.add),
                reads=(kc_b[pr],), writes=(ks_b,))

        def kv_tp(c16, jt):
            pr = c16 % 2
            kb0 = 0 if pr == 0 else 2
            tb = (c16 * NT + jt) % NTMP
            off = (tb % 2) * 256
            ctx.op("pe", lambda e, pr=pr, jt=jt, kb0=kb0, off=off: e.transpose(
                pbf(kb0)[:, off:off + 128],
                kc_sb[pr][:, jt * 128:(jt + 1) * 128], ident),
                reads=(kc_b[pr], ident_b), writes=(pbank[kb0],), sem="pe", inc=1)
            ctx.op("pe", lambda e, pr=pr, jt=jt, kb0=kb0, off=off: e.transpose(
                pbf(kb0)[:, off + 128:off + 256],
                vc_sb[pr][:, jt * 128:(jt + 1) * 128], ident),
                reads=(vc_b[pr], ident_b), writes=(pbank[kb0],), sem="pe", inc=1)
            vop(lambda e, tb=tb, kb0=kb0, off=off: e.tensor_copy(
                out=ktmp[tb], in_=pbf(kb0)[:, off:off + 128]),
                reads=(pbank[kb0],), writes=(ktmp_b[tb],))
            vop(lambda e, tb=tb, kb0=kb0, off=off: e.tensor_copy(
                out=vtmp[tb], in_=pbf(kb0)[:, off + 128:off + 256]),
                reads=(pbank[kb0],), writes=(vtmp_b[tb],))

        def kv_mm(c16, jt):
            tb = (c16 * NT + jt) % NTMP
            ctx.op("pe", lambda e, c16=c16, jt=jt, tb=tb: e.matmul(
                psum[:, 2048 + c16 * 128:2048 + (c16 + 1) * 128],
                ktmp[tb], vtmp[tb], start=(jt == 0), stop=(jt == NT - 1)),
                reads=(ktmp_b[tb], vtmp_b[tb]),
                writes=tuple(kvps) if (c16 == 0 and jt == 0) or
                (c16 == NH2 - 1 and jt == NT - 1) else (),
                sem="pe", inc=1)

        steps = []
        for c16 in range(NH2):
            steps.append(("prod", c16, None))
            for jt in range(NT):
                steps.append(("tp", c16, jt))
        mmq = []
        for kind, c16, jt in steps:
            if kind == "prod":
                kv_prod(c16)
            else:
                kv_tp(c16, jt)
                mmq.append((c16, jt))
                if len(mmq) > 2:
                    kv_mm(*mmq.pop(0))
        while mmq:
            kv_mm(*mmq.pop(0))

        kvpart = ar.alloc(NH2 * 64, F32)
        kvp_b = Buf("kvpart")
        kvsc = ar.alloc(NH2 * 64, F32)
        kvsc_b = Buf("kvsc")
        kssc = ar.alloc(16, F32)
        kssc_b = Buf("kssc")
        for c16 in range(NH2):
            vop(lambda e, c16=c16: e.tensor_copy(
                out=kvpart[0:64, c16 * 64:(c16 + 1) * 64],
                in_=psum[0:64, 2048 + c16 * 128:2048 + c16 * 128 + 64]),
                reads=(kvps[c16 // 4],), writes=(kvp_b,))
            vop(lambda e, c16=c16: e.tensor_copy(
                out=kvpart[64:128, c16 * 64:(c16 + 1) * 64],
                in_=psum[64:128, 2048 + c16 * 128 + 64:2048 + c16 * 128 + 128]),
                reads=(kvps[c16 // 4],), writes=(kvp_b,))
        kvcci_b = Buf("kvcci")
        kscci_b = Buf("kscci")
        for b in range(2):
            vop(lambda e, b=b: e.tensor_scalar(out=kvsc, in0=kvpart,
                scalar1=selc[:, b:b + 1], scalar2=None, op0=ALU.mult),
                reads=(kvp_b, selc_b), writes=(kvsc_b,))
            vop(lambda e, b=b: e.tensor_scalar(out=kssc, in0=ks_sb,
                scalar1=selc[:, b:b + 1], scalar2=None, op0=ALU.mult),
                reads=(ks_b, selc_b), writes=(kssc_b,))
            dma("gp", kv_cci[b * 2048:(b + 1) * 2048, :].rearrange(
                "(c p) e -> p c e", p=128),
                kvsc.rearrange("p (c e) -> p c e", e=64),
                reads=(kvsc_b,), writes=(kvcci_b,))
            dma("gp", ks_cci[b * 128:(b + 1) * 128, :], kssc,
                reads=(kssc_b,), writes=(kscci_b,))
        cc_b = Buf("kvar")
        coll(lambda e: e.collective_compute(
            "AllReduce", ALU.add, replica_groups=[list(range(NCORE))],
            ins=[kv_cci.tensor.ap().opt()], outs=[kv_cco.tensor.ap().opt()]),
            reads=(kvcci_b,), writes=(cc_b,))
        coll(lambda e: e.collective_compute(
            "AllReduce", ALU.add, replica_groups=[list(range(NCORE))],
            ins=[ks_cci.tensor.ap().opt()], outs=[ks_cco.tensor.ap().opt()]),
            reads=(kscci_b,), writes=(cc_b,))

        for mc in range(KC):
            buf = mc % 2
            wtile_dma("wqt", mc, buf)
            bank0 = 0 if mc % 2 == 0 else 2
            pw = [pbank[bank0], pbank[bank0 + 1]]
            for kc in range(KC):
                for q in range(2):
                    ctx.op("pe", lambda e, kc=kc, q=q, buf=buf, bank0=bank0,
                           xn2_fm=xn2_fm:
                           e.matmul(psum[:, (bank0 + q) * 512:(bank0 + q + 1) * 512],
                                    kv_ws[buf][:, kc * 128:(kc + 1) * 128],
                                    xn2_fm[kc][:, q * 512:(q + 1) * 512],
                                    start=(kc == 0), stop=(kc == KC - 1)),
                           reads=(kv_wsb[buf], xn2b[kc]),
                           writes=tuple(pw) if kc in (0, KC - 1) else (),
                           sem="pe", inc=1 if kc == KC - 1 else 0)
            ctx.op("sc", lambda e, mc=mc, bank0=bank0, q_fm=q_fm: e.activation(
                out=q_fm[mc], in_=psum[:, bank0 * 512:(bank0 + 2) * 512], func=AF.Relu),
                reads=tuple(pw), writes=(qfb[mc],), sem="sc", inc=1)

        kvld = [kvpart, kvsc]
        kvld_b = [kvp_b, kvsc_b]
        ksld = [ks_sb, kssc]
        ksld_b = [ks_b, kssc_b]
        for b in range(2):
            dma("gp", kvld[b].rearrange("p (c e) -> p c e", e=64),
                kv_cco[b * 2048:(b + 1) * 2048, :].rearrange("(c p) e -> p c e", p=128),
                reads=(cc_b,), writes=(kvld_b[b],))
            dma("gp", ksld[b], ks_cco[b * 128:(b + 1) * 128, :],
                reads=(cc_b,), writes=(ksld_b[b],))
        vop(lambda e: e.tensor_scalar(out=kvld[0], in0=kvld[0], scalar1=selc[:, 0:1],
            scalar2=None, op0=ALU.mult), reads=(kvld_b[0], selc_b), writes=(kvld_b[0],))
        vop(lambda e: e.scalar_tensor_tensor(out=kvld[0], in0=kvld[1],
            scalar=selc[:, 1:2], in1=kvld[0], op0=ALU.mult, op1=ALU.add),
            reads=(kvld_b[1], kvld_b[0], selc_b), writes=(kvld_b[0],))
        vop(lambda e: e.tensor_scalar(out=ksld[0], in0=ksld[0], scalar1=selc[:, 0:1],
            scalar2=None, op0=ALU.mult), reads=(ksld_b[0], selc_b), writes=(ksld_b[0],))
        vop(lambda e: e.scalar_tensor_tensor(out=ksld[0], in0=ksld[1],
            scalar=selc[:, 1:2], in1=ksld[0], op0=ALU.mult, op1=ALU.add),
            reads=(ksld_b[1], ksld_b[0], selc_b), writes=(ksld_b[0],))
        ctx.op("gp", lambda e: e.memset(kscols, 0.0), writes=(kscols_b,),
               sem="gp", inc=1)
        for c16 in range(NH2):
            ctx.op("gp", lambda e, c16=c16: e.memset(kvbd[c16], 0.0),
                   writes=(kvbd_b[c16],), sem="gp", inc=1)
            vop(lambda e, c16=c16: e.tensor_copy(
                out=kvbd[c16][0:64, 0:64], in_=kvld[0][0:64, c16 * 64:(c16 + 1) * 64]),
                reads=(kvld_b[0],), writes=(kvbd_b[c16],))
            vop(lambda e, c16=c16: e.tensor_copy(
                out=kvbd[c16][64:128, 64:128],
                in_=kvld[0][64:128, c16 * 64:(c16 + 1) * 64]),
                reads=(kvld_b[0],), writes=(kvbd_b[c16],))
            vop(lambda e, c16=c16: e.tensor_copy(
                out=kscols[0:64, 2 * c16:2 * c16 + 1], in_=ksld[0][0:64, c16:c16 + 1]),
                reads=(ksld_b[0],), writes=(kscols_b,))
            vop(lambda e, c16=c16: e.tensor_copy(
                out=kscols[64:128, 2 * c16 + 1:2 * c16 + 2],
                in_=ksld[0][64:128, c16:c16 + 1]),
                reads=(ksld_b[0],), writes=(kscols_b,))
        ar.pop()
        ctx.barrier()

        rn_all = ar.alloc(256, F32)
        rn_all_b = Buf("rn_all")
        for jt in range(NT):
            for c16 in range(NH2):
                first = (jt == 0 and c16 == 0)
                last = (jt == NT - 1 and c16 == NH2 - 1)
                ctx.op("pe", lambda e, c16=c16, jt=jt, q_fm=q_fm: e.matmul(
                    psum[:, 7 * 512 + jt * 32 + 2 * c16:
                         7 * 512 + jt * 32 + 2 * c16 + 2],
                    q_fm[c16][:, jt * 128:(jt + 1) * 128],
                    kscols[:, 2 * c16:2 * c16 + 2], start=True, stop=True),
                    reads=(qfb[c16], kscols_b),
                    writes=(pbank[7],) if (first or last) else (),
                    sem="pe", inc=1 if last else 0)
        vop(lambda e: e.tensor_scalar(out=rn_all, in0=psum[:, 7 * 512:7 * 512 + 256],
            scalar1=CLIP, scalar2=None, op0=ALU.max),
            reads=(pbank[7],), writes=(rn_all_b,))
        vop(lambda e: e.reciprocal(out=rn_all, in_=rn_all),
            reads=(rn_all_b,), writes=(rn_all_b,))

        qkvn2 = ar.alloc(D, BF16)
        qkvns = [(qkvn, Buf("qn0s"), Buf("qn0d")),
                 (qkvn2, Buf("qn1s"), Buf("qn1d"))]

        def qkv_mm(jt):
            g0 = 0 if jt % 2 == 0 else 4
            pq = [pbank[g0 + q] for q in range(4)]
            for c16 in range(NH2):
                ctx.op("pe", lambda e, c16=c16, jt=jt, g0=g0, q_fm=q_fm: e.matmul(
                    psum[:, g0 * 512 + c16 * 128:g0 * 512 + (c16 + 1) * 128],
                    q_fm[c16][:, jt * 128:(jt + 1) * 128], kvbd[c16],
                    start=True, stop=True),
                    reads=(qfb[c16], kvbd_b[c16]),
                    writes=tuple(pq) if c16 in (0, NH2 - 1) else (),
                    sem="pe", inc=1 if c16 == NH2 - 1 else 0)
            qn, qnb_s, qnb_d = qkvns[jt % 2]
            for h in range(32):
                if h < 8:
                    ctx.op("sc", lambda e, h=h, g0=g0, jt=jt, qn=qn: e.activation(
                        out=qn[:, h * 64:(h + 1) * 64],
                        in_=psum[:, g0 * 512 + h * 64:g0 * 512 + (h + 1) * 64],
                        func=AF.Copy,
                        scale=rn_all[:, jt * 32 + h:jt * 32 + h + 1]),
                        reads=(rn_all_b,) + tuple(pq), writes=(qnb_s,),
                        sem="sc", inc=1)
                else:
                    vop(lambda e, h=h, g0=g0, jt=jt, qn=qn: e.tensor_scalar(
                        out=qn[:, h * 64:(h + 1) * 64],
                        in0=psum[:, g0 * 512 + h * 64:g0 * 512 + (h + 1) * 64],
                        scalar1=rn_all[:, jt * 32 + h:jt * 32 + h + 1],
                        scalar2=None, op0=ALU.mult),
                        reads=(rn_all_b,) + tuple(pq), writes=(qnb_d,))

        def qkv_tr(jt):
            g0 = 0 if jt % 2 == 0 else 4
            qn, qnb_s, qnb_d = qkvns[jt % 2]
            for k4 in range(0, KC, 4):
                for t in range(4):
                    ec = k4 + t
                    ctx.op("pe", lambda e, ec=ec, t=t, g0=g0, qn=qn: e.transpose(
                        pbf(g0)[:, t * 128:(t + 1) * 128],
                        qn[:, ec * 128:(ec + 1) * 128], ident),
                        reads=(qnb_s, qnb_d, ident_b), writes=(pbank[g0],),
                        sem="pe", inc=1)
                for t in range(4):
                    ec = k4 + t
                    vop(lambda e, ec=ec, t=t, jt=jt, g0=g0, qkvn_fm=qkvn_fm:
                        e.tensor_copy(
                            out=qkvn_fm[ec][:, jt * 128:(jt + 1) * 128],
                            in_=pbf(g0)[:, t * 128:(t + 1) * 128]),
                        reads=(pbank[g0],), writes=(qkvn_fmb[ec],))

        qkv_mm(0)
        for jt in range(1, NT):
            qkv_mm(jt)
            qkv_tr(jt - 1)
        qkv_tr(NT - 1)
        ar.pop()
        ctx.barrier()

        ar.push()
        wo_sb = [ar.alloc(D, BF16) for _ in range(KC)]
        wo_b = [Buf(f"wo{e}") for e in range(KC)]
        for ec in range(KC):
            dma("sy", wo_sb[ec],
                w_lin_all.rearrange("n (p f) -> n p f", p=128)[w_lin_row("wot", ec)],
                reads=(attn_lin_b,), writes=(wo_b[ec],))
        for jt in range(NT):
            g0 = 0 if jt % 2 == 0 else 4
            po = [pbank[g0 + q] for q in range(4)]
            for ec in range(KC):
                for q in range(4):
                    ctx.op("pe", lambda e, ec=ec, q=q, jt=jt, g0=g0, qkvn_fm=qkvn_fm,
                           wo_sb=wo_sb: e.matmul(
                        psum[:, (g0 + q) * 512:(g0 + q + 1) * 512],
                        qkvn_fm[ec][:, jt * 128:(jt + 1) * 128],
                        wo_sb[ec][:, q * 512:(q + 1) * 512],
                        start=(ec == 0), stop=False),
                        reads=(qkvn_fmb[ec], wo_b[ec]),
                        writes=tuple(po) if ec == 0 else (),
                        sem="pe", inc=0)
            for q in range(4):
                ctx.op("pe", lambda e, q=q, g0=g0: e.matmul(
                    psum[:, (g0 + q) * 512:(g0 + q + 1) * 512], ones_row[0:1, :],
                    bo_bf[0:1, q * 512:(q + 1) * 512], start=False, stop=True),
                    reads=(ones_b, bo_bb),
                    writes=tuple(po) if q == 3 else (),
                    sem="pe", inc=1 if q == 3 else 0)
            vop(lambda e, jt=jt, g0=g0: e.tensor_tensor(
                out=x_tm[jt], in0=psum[:, g0 * 512:g0 * 512 + 2048],
                in1=x_tm[jt], op=ALU.add), reads=(xb[jt],) + tuple(po),
                writes=(xb[jt],))
        ar.pop()
        ar.pop()

    ctx.barrier()
    if "ffn1" in phases:
        emit_ffn([0, 512], dodump=True, with_jobs=True)
        ctx.barrier()
    if "attn" in phases:
        emit_attn()
        ctx.barrier()
    if "ffn3" in phases:
        emit_ffn([0, 512], nwg=9)
        ctx.barrier()
    for j in range(NT):
        dma("gp", y_out[j * 128:(j + 1) * 128, :], x_tm[j], reads=(xb[j],),
            writes=(Buf(f"y{j}"),))
    for i in range(8):
        ctx._wait("gp", f"qgp{i}", ctx.counts[f"qgp{i}"])
    ctx.emit()
    return nc


def make_in_maps(L0_attn_hidden_states, L0_attn_Wq, L0_attn_Wk, L0_attn_Wv,
                 L0_attn_Wo, L0_attn_bo, L0_ffn_gate_up_packed,
                 L0_ffn_gate_up_scales, L0_ffn_down_packed, L0_ffn_down_scales):
    x = np.ascontiguousarray(np.asarray(L0_attn_hidden_states, dtype=np.float32)
                             .reshape(NCORE * T, D))
    wq = np.ascontiguousarray(np.asarray(L0_attn_Wq, dtype=np.float32))
    wk = np.ascontiguousarray(np.asarray(L0_attn_Wk, dtype=np.float32))
    wv = np.ascontiguousarray(np.asarray(L0_attn_Wv, dtype=np.float32))
    wo = np.ascontiguousarray(np.asarray(L0_attn_Wo, dtype=np.float32))
    bo = np.ascontiguousarray(np.asarray(L0_attn_bo, dtype=np.float32)).reshape(1, D)
    gup = np.ascontiguousarray(np.asarray(L0_ffn_gate_up_packed, dtype=np.int32))
    gus = np.ascontiguousarray(np.asarray(L0_ffn_gate_up_scales, dtype=np.float16).T)
    dnp = np.ascontiguousarray(np.asarray(L0_ffn_down_packed, dtype=np.int32))
    dns = np.ascontiguousarray(np.asarray(L0_ffn_down_scales, dtype=np.float16).T)
    ic_cnt = [6, 6, 6, 6, 5, 5, 5, 5]
    ic_start = [0, 6, 12, 18, 24, 29, 34, 39]
    in_maps = []
    for c in range(NCORE):
        sel = np.zeros((1, 2), np.float32)
        sel[0, 0 if c < 4 else 1] = 1.0
        s0, cnt = ic_start[c], ic_cnt[c]
        dnp_c = np.zeros((D, 96), np.int32)
        dnp_c[:, 0:cnt * 16] = dnp[:, s0 * 16:(s0 + cnt) * 16]
        dns_c = np.zeros((D, 6), np.float16)
        dns_c[:, 0:cnt] = dns[:, s0:s0 + cnt]
        in_maps.append({
            "x_in": x[c * T:(c + 1) * T],
            "gup": gup[c * 1408:(c + 1) * 1408],
            "gus": gus[c * 1408:(c + 1) * 1408],
            "dnp": np.ascontiguousarray(dnp_c),
            "dns": np.ascontiguousarray(dns_c),
            "wq": np.ascontiguousarray(wq[:, c * 256:(c + 1) * 256]),
            "wk": np.ascontiguousarray(wk[:, c * 256:(c + 1) * 256]),
            "wv": np.ascontiguousarray(wv[:, c * 256:(c + 1) * 256]),
            "wo": np.ascontiguousarray(wo[c * 256:(c + 1) * 256]),
            "bo": bo,
            "selb": sel,
        })
    return in_maps


_NC_CACHE = None


def kernel(L0_attn_hidden_states, L0_attn_Wq, L0_attn_Wk, L0_attn_Wv, L0_attn_Wo,
           L0_attn_bo, L0_ffn_gate_up_packed, L0_ffn_gate_up_scales,
           L0_ffn_down_packed, L0_ffn_down_scales):
    global _NC_CACHE
    in_maps = make_in_maps(L0_attn_hidden_states, L0_attn_Wq, L0_attn_Wk, L0_attn_Wv,
                           L0_attn_Wo, L0_attn_bo, L0_ffn_gate_up_packed,
                           L0_ffn_gate_up_scales, L0_ffn_down_packed,
                           L0_ffn_down_scales)
    if _NC_CACHE is None:
        _NC_CACHE = build_program()
    nc = _NC_CACHE
    res = run_bass_kernel_spmd(nc, in_maps, core_ids=list(range(NCORE)))
    out = np.concatenate([r["y"] for r in res.results], axis=0)
    return out.reshape(2, 4096, 2048)
